# version 12
# baseline (speedup 1.0000x reference)
import numpy as np
import contextlib
import concourse.bass as bass
import concourse.mybir as mybir
from concourse.bass_utils import run_bass_kernel_spmd

F32 = mybir.dt.float32
BF16 = mybir.dt.bfloat16
I32 = mybir.dt.int32
AF = mybir.ActivationFunctionType
OP = mybir.AluOpType
AX = mybir.AxisListType

T_FULL = 4096
C = 1024
TC = 512
NCH = T_FULL // TC
L = 128
NL = TC // L
NTT = TC // 128
NFT = C // 128
H = 16
HN = 64
FFN = 3584
FH = 4096
TD = 64
TMDIM = 160
LN_EPS = 1e-5
GN_EPS = 0.00064

DEBUG = False
DUMP_CH = None
DUMP_KEYS = None
LAZY_ENGINES = ('V',)
PHASES = 'ABCDEFGHI'
CLEV = 9


class Sched:
    def __init__(self):
        self.ops = []
        self.cnt = {e: 0 for e in
                    ["T", "V", "S", "G"]
                    + [f"Y{i}" for i in range(8)] + [f"D{i}" for i in range(8)]}
        self.dmarr = {"Y": -1, "D": -1}
        self.writer = {}
        self.readers = {}
        self.pending_barrier = None
        self.barrier_seen = set()

    def op(self, eng, fn, reads=(), writes=(), dma=False, lazy=False):
        if dma:
            q = "D" if eng == "G" else "Y"
            self.dmarr[q] = (self.dmarr[q] + 1) % 8
            chan = f"{q}{self.dmarr[q]}"
        else:
            chan = eng
        waits = {}

        def add(ev):
            e2, c2, isdma = ev
            if e2 == chan and not isdma:
                return
            waits[e2] = max(waits.get(e2, 0), c2)

        for r in reads:
            w = self.writer.get(r)
            if w is not None:
                add(w)
        for wb in writes:
            w = self.writer.get(wb)
            if w is not None:
                add(w)
            for ev in self.readers.get(wb, ()):
                add(ev)
        if dma and self.cnt[chan] > 0:
            waits[chan] = max(waits.get(chan, 0), self.cnt[chan])
        inc = 16 if dma else 1
        self.cnt[chan] += inc
        ev = (chan, self.cnt[chan], dma or lazy or (eng in LAZY_ENGINES))
        for r in reads:
            self.readers.setdefault(r, []).append(ev)
        for wb in writes:
            self.writer[wb] = ev
            self.readers[wb] = []
        if self.pending_barrier:
            for e2, c2 in self.pending_barrier.items():
                if e2 == chan and not dma:
                    continue
                if c2 > 0:
                    waits[e2] = max(waits.get(e2, 0), c2)
            if eng in self.barrier_seen:
                pass
            self.barrier_seen.add(eng)
        self.ops.append((eng, fn, sorted(waits.items()), inc, chan))

    def barrier(self):
        self.pending_barrier = dict(self.cnt)
        self.barrier_seen = set()


def ts(i, n=128):
    return slice(i * n, (i + 1) * n)


def build_nc():
    nc = bass.Bass("TRN2", target_bir_lowering=False, debug=False,
                   detect_race_conditions=False)

    def din(name, shape, dt=F32):
        return nc.dram_tensor(name, shape, dt, kind="ExternalInput").ap()

    x_in = din("x", [T_FULL, C])
    ln0_w = din("ln0_w", [C]); ln0_b = din("ln0_b", [C])
    ln1_w = din("ln1_w", [C]); ln1_b = din("ln1_b", [C])
    ln2_w = din("ln2_w", [C]); ln2_b = din("ln2_b", [C])
    maa = {k: din(k + "_maa", [C]) for k in ("x", "w", "k", "v", "r", "g")}
    tm_w1 = din("tm_w1", [C, TMDIM])
    tm_w2 = din("tm_w2", [5, 32, C])
    td_w1 = din("td_w1", [C, TD])
    td_w2 = din("td_w2", [TD, C])
    time_decay = din("time_decay", [C])
    time_faaaa = din("time_faaaa", [H, HN])
    Wr = din("Wr", [C, C]); Wk = din("Wk", [C, C]); Wv = din("Wv", [C, C])
    Wg = din("Wg", [C, C]); Wo = din("Wo", [C, C])
    lnx_w = din("lnx_w", [C]); lnx_b = din("lnx_b", [C])
    fk_maa = din("fk_maa", [C]); fr_maa = din("fr_maa", [C])
    Wk_f = din("Wk_f", [C, FFN]); Wv_f = din("Wv_f", [FFN, C])
    Wr_f = din("Wr_f", [C, C])
    ffn_g = din("ffn_g", [C]); ffn_b = din("ffn_b", [C])
    fc1_q = din("fc1_q", [C, FH], I32)
    fc1_bias = din("fc1_bias", [FH])
    fc2_q = din("fc2_q", [FH, C], I32)
    fc2_bias = din("fc2_bias", [C])
    cmaskQ = din("cmaskQ", [128, 512], BF16)
    cident = din("cident", [128, 128], BF16)
    cw0bc = din("cw0bc", [128, C]); cb0bc = din("cb0bc", [128, C])
    clnxw = din("clnxw", [128, C]); clnxb = din("clnxb", [128, C])
    cfc2bb = din("cfc2bb", [128, C])
    cfc1s = din("cfc1s", [C, FH])
    cfc2s = din("cfc2s", [FH, C])

    s_fc1 = nc.dram_tensor("s_fc1", [C, FH], BF16, kind="Internal").ap()
    s_fc2 = nc.dram_tensor("s_fc2", [FH, C], BF16, kind="Internal").ap()
    s_wkf = nc.dram_tensor("s_wkf", [C, FFN], BF16, kind="Internal").ap()
    s_wvf = nc.dram_tensor("s_wvf", [FFN, C], BF16, kind="Internal").ap()
    s_w6 = nc.dram_tensor("s_w6", [6 * C, C], BF16, kind="Internal").ap()

    y_out = nc.dram_tensor("y", [T_FULL, C], F32, kind="ExternalOutput").ap()

    dbg = {}
    if DEBUG:
        for nm, shape in [
            ("d_n0", [TC, C]), ("d_h0", [TC, C]), ("d_xa", [C, TC]),
            ("d_xx", [C, TC]), ("d_xw", [C, TC]), ("d_xk", [C, TC]),
            ("d_r", [C, TC]), ("d_k", [C, TC]), ("d_v", [TC, C]),
            ("d_ew", [C, TC]), ("d_g", [TC, C]), ("d_y3", [C, TC]),
            ("d_att", [TC, C]), ("d_xc", [C, TC]), ("d_kx", [C, TC]),
            ("d_sig", [TC, C]), ("d_kk", [FFN, TC]), ("d_hf", [C, TC]),
            ("d_d2", [TC, C]), ("d_accH", [TC, C]), ("d_y1", [FH, TC]),
            ("d_accA", [TC, C]), ("d_st", [128, 16]),
            ("d_cs", [128, 1024]), ("d_F1", [128, 1024]), ("d_F2", [128, 1024]),
            ("d_Kh", [128, 1024]), ("d_Rt", [128, 1024]), ("d_ktm", [128, 1024]),
            ("d_mt", [128, 2048]), ("d_Sf", [128, 512]), ("d_gn", [128, 64]),
            ("d_y3l", [128, 1024]), ("d_ysq", [128, 1024]),
        ]:
            dbg[nm] = nc.dram_tensor(nm, shape, F32, kind="ExternalOutput").ap()

    es = contextlib.ExitStack()
    ec = es.enter_context

    U = 4096
    NUNITS = 14
    bfa = ec(nc.sbuf_tensor("bfa", [128, NUNITS * U], BF16))

    def unit(i, j=None):
        j = i + 1 if j is None else j
        return bfa[:, i * U:j * U]

    U_NRM, U_N0N, U_H0, U_HF = 0, 1, 2, 3
    U_XA, U_XX, U_XF0, U_XF1 = 4, 5, 6, 7
    U_XG, U_R, U_K, U_V, U_G, U_EW = 8, 9, 10, 11, 12, 13

    def K(i):
        return f"U{i}"

    f32a = ec(nc.sbuf_tensor("f32a", [128, 4096], F32))
    acc = ec(nc.sbuf_tensor("acc", [128, NTT * C], F32))
    S_f = ec(nc.sbuf_tensor("S_f", [128, 512], F32))
    S_b = ec(nc.sbuf_tensor("S_b", [128, 512], BF16))
    mtdd = ec(nc.sbuf_tensor("mtdd", [128, 2048], BF16))
    ysq = mtdd[:, 0:1024]
    sfb = mtdd[:, 0:512]
    t1b = ec(nc.sbuf_tensor("t1b", [64, 512], BF16))
    zb = ec(nc.sbuf_tensor("zb", [128, 1024], BF16))
    z0b = zb[:, 0:512]
    z1b = zb[0:32, 512:1024]
    arena = [ec(nc.sbuf_tensor(f"arena{i}", [128, 8192], BF16)) for i in range(2)]

    maskQ = ec(nc.sbuf_tensor("maskQ", [128, 512], BF16))
    ident = ec(nc.sbuf_tensor("ident", [128, 128], BF16))
    w0bc = ec(nc.sbuf_tensor("w0bc", [128, C], BF16))
    b0bc = ec(nc.sbuf_tensor("b0bc", [128, C], BF16))
    lnxwbc = ec(nc.sbuf_tensor("lnxwbc", [128, C], BF16))
    lnxbbc = ec(nc.sbuf_tensor("lnxbbc", [128, C], BF16))
    tmw1 = ec(nc.sbuf_tensor("tmw1", [128, NFT * TMDIM], BF16))
    w2a = ec(nc.sbuf_tensor("w2a", [128, C], BF16))
    w2b = ec(nc.sbuf_tensor("w2b", [32, C], BF16))
    tdw1 = ec(nc.sbuf_tensor("tdw1", [128, NFT * TD], BF16))
    tdw2 = ec(nc.sbuf_tensor("tdw2", [64, C], BF16))
    ones128 = ec(nc.sbuf_tensor("ones128", [128, 128], BF16))

    VC = {}
    vec_srcs = {
        "ln1w": ln1_w, "ln1b": ln1_b, "ln2w": ln2_w, "ln2b": ln2_b,
        "xmaa": maa["x"], "wmaa": maa["w"], "kmaa": maa["k"],
        "vmaa": maa["v"], "rmaa": maa["r"], "gmaa": maa["g"],
        "tdec": time_decay, "fkmaa": fk_maa, "frmaa": fr_maa,
        "ffng": ffn_g, "ffnb": ffn_b,
    }
    for nm in list(vec_srcs) + ["ucol", "fc1b"]:
        w = FH // 128 if nm == "fc1b" else NFT
        VC[nm] = ec(nc.sbuf_tensor("vc_" + nm, [128, w], F32))

    bn6 = ec(nc.sbuf_tensor("bn6", [128, 12], F32))
    mv4 = ec(nc.sbuf_tensor("mv4", [128, 2 * NTT], F32))
    rs4 = ec(nc.sbuf_tensor("rs4", [128, NTT], F32))
    nb4 = ec(nc.sbuf_tensor("nb4", [128, NTT], F32))
    rs4b = ec(nc.sbuf_tensor("rs4b", [128, NTT], F32))
    gn16 = ec(nc.sbuf_tensor("gn16", [128, 64], F32))
    AL8 = ec(nc.sbuf_tensor("AL8", [128, NFT], F32))
    xacar = ec(nc.sbuf_tensor("xacar", [128, NFT], BF16))
    epsln = ec(nc.sbuf_tensor("epsln", [128, 1], F32))
    epsgn = ec(nc.sbuf_tensor("epsgn", [128, 1], F32))
    b4k = ec(nc.sbuf_tensor("b4k", [128, 1], F32))
    xccar = ec(nc.sbuf_tensor("xccar", [128, NFT], BF16))

    QD0 = ec(nc.psum_tensor("QD0", [128, 512], F32))
    QD1 = ec(nc.psum_tensor("QD1", [128, 512], F32))
    QD2 = ec(nc.psum_tensor("QD2", [128, 512], F32))
    QD3 = ec(nc.psum_tensor("QD3", [128, 512], F32))
    PST = ec(nc.psum_tensor("PST", [128, 1024], BF16))
    YY = ec(nc.psum_tensor("YY", [128, 1024], F32))
    SS = ec(nc.psum_tensor("SS", [128, 512], F32))

    sems = {e: ec(nc.semaphore("sem" + e))
            for e in ["T", "V", "S", "G"]
            + [f"Y{i}" for i in range(8)] + [f"D{i}" for i in range(8)]}

    sc = Sched()

    def prologue():
        sc.op("V", lambda e: e.memset(ones128[:], 1.0), writes=["ones"])
        sc.op("V", lambda e: e.memset(epsln[:], LN_EPS), writes=["epsln"])
        sc.op("V", lambda e: e.memset(epsgn[:], GN_EPS * HN * HN),
              writes=["epsgn"])
        sc.op("V", lambda e: e.memset(b4k[:], 0.5 * np.log(float(HN * HN))),
              writes=["b4k"])
        sc.op("G", lambda e: e.memset(S_f[:], 0.0), writes=["S_f"])
        sc.op("G", lambda e: e.memset(S_b[:], 0.0), writes=["S_b"])
        sc.op("G", lambda e: e.memset(xacar[:], 0.0), writes=["xacar"])
        sc.op("G", lambda e: e.memset(xccar[:], 0.0), writes=["xccar"])
        sc.op("Y", lambda e: e.dma_start(maskQ[:], cmaskQ), writes=["maskQ"], dma=True)
        sc.op("Y", lambda e: e.dma_start(ident[:], cident), writes=["ident"], dma=True)
        sc.op("G", lambda e: e.dma_start(w0bc[:], cw0bc), writes=["w0bc"], dma=True)
        sc.op("G", lambda e: e.dma_start(b0bc[:], cb0bc), writes=["b0bc"], dma=True)
        sc.op("G", lambda e: e.dma_start(lnxwbc[:], clnxw), writes=["lnxw"], dma=True)
        sc.op("G", lambda e: e.dma_start(lnxbbc[:], clnxb), writes=["lnxb"], dma=True)
        for nm, src in vec_srcs.items():
            ap = src.rearrange("(t p) -> p t", p=128)
            sc.op("Y", lambda e, ap=ap, nm=nm: e.dma_start(
                VC[nm][:], ap, allow_slow_non_contiguous=True),
                writes=["vc_" + nm], dma=True)
        apu = time_faaaa.rearrange("h n -> (h n)").rearrange("(t p) -> p t", p=128)
        sc.op("Y", lambda e: e.dma_start(VC["ucol"][:], apu,
                                         allow_slow_non_contiguous=True),
              writes=["vc_ucol"], dma=True)
        apb = fc1_bias.rearrange("(t p) -> p t", p=128)
        sc.op("Y", lambda e: e.dma_start(VC["fc1b"][:], apb,
                                         allow_slow_non_contiguous=True),
              writes=["vc_fc1b"], dma=True)
        sc.op("G", lambda e: e.dma_start(
            tmw1[:].rearrange("p (kt m) -> p kt m", kt=NFT),
            tm_w1.rearrange("(kt p) m -> p kt m", p=128)),
            writes=["tmw1"], dma=True)
        for f in range(4):
            sc.op("G", lambda e, f=f: e.dma_start(w2a[ts(f, 32), :], tm_w2[f]),
                  writes=["w2a"], dma=True)
        sc.op("G", lambda e: e.dma_start(w2b[:], tm_w2[4]), writes=["w2b"], dma=True)
        sc.op("G", lambda e: e.dma_start(
            tdw1[:].rearrange("p (kt m) -> p kt m", kt=NFT),
            td_w1.rearrange("(kt p) m -> p kt m", p=128)),
            writes=["tdw1"], dma=True)
        sc.op("G", lambda e: e.dma_start(tdw2[:], td_w2), writes=["tdw2"], dma=True)

        for wi, wsrc in enumerate((Wk, Wv, Wr, Wg, Wo, Wr_f)):
            sc.op("G", lambda e, wsrc=wsrc: e.dma_start(
                arena[0][:, :8192].rearrange("p (kt n) -> p kt n", kt=NFT),
                wsrc.rearrange("(kt p) n -> p kt n", p=128)),
                writes=["slot0"], dma=True)
            sc.op("Y", lambda e, wi=wi: e.dma_start(
                s_w6[wi * C:(wi + 1) * C, :]
                .rearrange("(kt p) n -> p kt n", p=128),
                arena[0][:, :8192].rearrange("p (kt n) -> p kt n", kt=NFT)),
                reads=["slot0"], writes=["s_w6"], dma=True)
        stg = unit(U_EW)[:, :FFN]
        for kt in range(NFT):
            sc.op("G", lambda e, kt=kt: e.dma_start(stg, Wk_f[ts(kt), :]),
                  writes=[K(U_EW)], dma=True)
            sc.op("Y", lambda e, kt=kt: e.dma_start(s_wkf[ts(kt), :], stg),
                  reads=[K(U_EW)], writes=["s_wkf"], dma=True)
        stg2 = unit(U_EW)[:, :C]
        for kt in range(FFN // 128):
            sc.op("G", lambda e, kt=kt: e.dma_start(stg2, Wv_f[ts(kt), :]),
                  writes=[K(U_EW)], dma=True)
            sc.op("Y", lambda e, kt=kt: e.dma_start(s_wvf[ts(kt), :], stg2),
                  reads=[K(U_EW)], writes=["s_wvf"], dma=True)
        qv = unit(0, 2).bitcast(I32)
        sv = unit(2, 4).bitcast(F32)
        qf = unit(4, 6).bitcast(F32)
        ov = unit(6)[:, :FH]
        kq = [K(0), K(1)]
        ks = [K(2), K(3)]
        kf = [K(4), K(5)]
        ko = [K(6)]
        for kt in range(NFT):
            sc.op("Y", lambda e, kt=kt: e.dma_start(qv[:, :FH], fc1_q[ts(kt), :]),
                  writes=kq, dma=True)
            sc.op("Y", lambda e, kt=kt: e.dma_start(sv[:, :FH], cfc1s[ts(kt), :]),
                  writes=ks, dma=True)
            sc.op("V", lambda e: e.tensor_scalar_add(qf[:, :FH], qv[:, :FH], -128.0),
                  reads=kq, writes=kf)
            sc.op("V", lambda e: e.tensor_tensor(ov, qf[:, :FH], sv[:, :FH], OP.mult),
                  reads=kf + ks, writes=ko)
            sc.op("Y", lambda e, kt=kt: e.dma_start(s_fc1[ts(kt), :], ov),
                  reads=ko, writes=["s_fc1"], dma=True)
        for kt in range(FH // 128):
            sc.op("Y", lambda e, kt=kt: e.dma_start(qv[:, :C], fc2_q[ts(kt), :]),
                  writes=kq, dma=True)
            sc.op("Y", lambda e, kt=kt: e.dma_start(sv[:, :C], cfc2s[ts(kt), :]),
                  writes=ks, dma=True)
            sc.op("V", lambda e: e.tensor_scalar_add(qf[:, :C], qv[:, :C], -128.0),
                  reads=kq, writes=kf)
            sc.op("V", lambda e: e.tensor_tensor(
                ov[:, :C], qf[:, :C], sv[:, :C], OP.mult),
                reads=kf + ks, writes=ko)
            sc.op("Y", lambda e, kt=kt: e.dma_start(s_fc2[ts(kt), :], ov[:, :C]),
                  reads=ko, writes=["s_fc2"], dma=True)

    piece_defs = []
    for wi, nm in enumerate(("Wk", "Wv", "Wr", "Wg", "Wo", "Wrf")):
        piece_defs.append((nm, "G",
                           s_w6[wi * C:(wi + 1) * C, :]
                           .rearrange("(kt p) n -> p kt n", p=128), 8192))
    for q in range(4):
        piece_defs.append((f"wkf{q}", "Y",
                           s_wkf[:, q * 896:(q + 1) * 896]
                           .rearrange("(kt p) n -> p kt n", p=128), 7168))
    for hh in range(2):
        for q in range(4):
            piece_defs.append((f"wvf{q}h{hh}", "Y",
                               s_wvf[q * 896:(q + 1) * 896, :]
                               .rearrange("(kt p) n -> p kt n", p=128), 7168))
    for q in range(4):
        piece_defs.append((f"fc1{q}", "Y",
                           s_fc1[:, q * C:(q + 1) * C]
                           .rearrange("(kt p) n -> p kt n", p=128), 8192))
    for hh in range(2):
        for q in range(4):
            piece_defs.append((f"fc2{q}h{hh}", "Y",
                               s_fc2[q * C:(q + 1) * C, :]
                               .rearrange("(kt p) n -> p kt n", p=128), 8192))
    NPIECE = len(piece_defs)
    use_order = (["Wk", "Wv", "Wr", "Wg"]
                 + [f"fc1{q}" for q in range(4)]
                 + ["Wo"]
                 + [f"fc2{q}h{hh}" for hh in range(2) for q in range(4)]
                 + ["Wrf"]
                 + [f"wkf{q}" for q in range(4)]
                 + [f"wvf{q}h{hh}" for hh in range(2) for q in range(4)])
    pd = {nm: (qq, ap, cols) for nm, qq, ap, cols in piece_defs}
    piece_slot = {}
    seq = [(ch, nm) for ch in range(NCH) for nm in use_order]
    seq_pos = {key: i for i, key in enumerate(seq)}

    def _load(key):
        if key in piece_slot:
            return
        ch, nm = key
        qq, ap, cols = pd[nm]
        qq = "G" if seq_pos[key] % 2 == 0 else "Y"
        slot = seq_pos[key] % 2
        piece_slot[key] = slot
        nkt = ap.shape[1]
        dst = arena[slot][:, :cols].rearrange("p (kt n) -> p kt n", kt=nkt)
        rds = []
        for pref, sk in (("wkf", "s_wkf"), ("wvf", "s_wvf"),
                         ("fc1", "s_fc1"), ("fc2", "s_fc2"),
                         ("W", "s_w6")):
            if nm.startswith(pref):
                rds = [sk]
        sc.op(qq, lambda e, ap=ap, dst=dst: e.dma_start(dst, ap),
              reads=rds, writes=[f"slot{slot}"], dma=True)

    def piece(ch, nm):
        key = (ch, nm)
        _load(key)
        nxt = seq_pos[key] + 1
        if nxt < len(seq):
            _load(seq[nxt])
        slot = piece_slot[key]
        return arena[slot], f"slot{slot}"

    def ln_stats(src, srcbuf):
        for tt in range(NTT):
            a = src[:, tt * C: tt * C + 512]
            b = src[:, tt * C + 512: (tt + 1) * C]
            sc.op("V", lambda e, a=a: e.bn_stats(bn6[:, 0:6], a),
                  reads=[srcbuf], writes=["bn6"], lazy=True)
            sc.op("V", lambda e, b=b: e.bn_stats(bn6[:, 6:12], b),
                  reads=[srcbuf], writes=["bn6"], lazy=True)
            sc.op("V", lambda e, tt=tt: e.bn_aggr(mv4[:, 2 * tt:2 * tt + 2],
                                                  bn6[:]),
                  reads=["bn6"], writes=["mv4"], lazy=True)
        var2 = mv4[:].rearrange("p (t two) -> p t two", two=2)[:, :, 1:2] \
            .rearrange("p t one -> p (t one)")
        mu2 = mv4[:].rearrange("p (t two) -> p t two", two=2)[:, :, 0:1] \
            .rearrange("p t one -> p (t one)")
        sc.op("V", lambda e, var2=var2: e.tensor_copy(rs4[:], var2),
              reads=["mv4"], writes=["rs4"], lazy=True)
        sc.op("S", lambda e: e.activation(rs4[:], rs4[:], AF.Ln,
                                          bias=epsln[:, 0:1]),
              reads=["rs4", "epsln"], writes=["rs4"])
        sc.op("S", lambda e: e.activation(rs4[:], rs4[:], AF.Exp, scale=-0.5),
              reads=["rs4"], writes=["rs4"])
        sc.op("V", lambda e, mu2=mu2: e.scalar_tensor_tensor(
            nb4[:], mu2, -1.0, rs4[:], OP.mult, OP.mult),
            reads=["mv4", "rs4"], writes=["nb4"])

    def transpose_affine(src_tm, srcbuf, dst_fm, dstbuf, scale_col, bias_col,
                         scalebuf=None, biasbuf=None):
        for tt in range(NTT):
            for ft in range(NFT):
                sc.op("T", lambda e, tt=tt, ft=ft, src_tm=src_tm: e.transpose(
                    PST[:, ts(ft)],
                    src_tm[:, tt * C + ft * 128: tt * C + (ft + 1) * 128],
                    ident[:]),
                    reads=[srcbuf, "ident"], writes=["PST"])
            for ft in range(NFT):
                d = dst_fm[:, ft * TC + tt * 128: ft * TC + (tt + 1) * 128]
                s0 = scale_col[:, ft:ft + 1] if scale_col is not None else 1.0
                b0 = bias_col[:, ft:ft + 1] if bias_col is not None else 0.0
                rds = ["PST"] + ([scalebuf] if scalebuf else []) + (
                    [biasbuf] if biasbuf else [])
                sc.op("S", lambda e, d=d, ft=ft, s0=s0, b0=b0: e.activation(
                    d, PST[:, ts(ft)], AF.Identity, scale=s0, bias=b0),
                    reads=rds, writes=[dstbuf])

    def dump_fm(name, src, srcbuf, ch=None):
        if not DEBUG or (DUMP_CH is not None and ch is not None
                         and ch != DUMP_CH):
            return
        if DUMP_KEYS is not None and name not in DUMP_KEYS:
            return
        sc.op("G", lambda e, src=src, name=name: e.dma_start(
            dbg[name].rearrange("(ft p) t -> p ft t", p=128),
            src.rearrange("p (ft t) -> p ft t", t=TC)),
            reads=[srcbuf], dma=True)

    def dump_tm(name, src, srcbuf, ch=None):
        if not DEBUG or (DUMP_CH is not None and ch is not None
                         and ch != DUMP_CH):
            return
        if DUMP_KEYS is not None and name not in DUMP_KEYS:
            return
        sc.op("G", lambda e, src=src, name=name: e.dma_start(
            dbg[name].rearrange("(tt p) c -> p tt c", p=128),
            src.rearrange("p (tt c) -> p tt c", c=C)),
            reads=[srcbuf], dma=True)

    def chunk(ch):
        t0g = ch * TC
        _load(seq[ch * len(use_order)])

        x = f32a[:, 0:4096]
        y1units = [U_NRM, U_XX, U_XF1, U_XG]

        def y1sl(gkt, tt=None):
            un = y1units[gkt // 8]
            base = un * U + (gkt % 8) * TC
            if tt is None:
                return bfa[:, base:base + TC], K(un)
            return bfa[:, base + tt * 128: base + (tt + 1) * 128], K(un)
        sc.op("G", lambda e: e.dma_start(
            x.rearrange("p (tt c) -> p tt c", tt=NTT),
            x_in[t0g:t0g + TC, :].rearrange("(tt p) c -> p tt c", p=128)),
            writes=["f32a", "f32cs", "f32gn"], dma=True)
        ln_stats(x, "f32a")
        sc.op("V", lambda e: e.tensor_scalar_add(rs4b[:], rs4[:], 1.0),
              reads=["rs4"], writes=["rs4b"], lazy=True)
        n0 = unit(U_NRM)
        for tt in range(NTT):
            sc.op("S", lambda e, tt=tt: e.activation(
                acc[:, ts(tt, C)], x[:, ts(tt, C)], AF.Identity,
                scale=rs4b[:, tt:tt + 1], bias=nb4[:, tt:tt + 1]),
                reads=["f32a", "rs4b", "nb4"], writes=["acc"])
            sc.op("S", lambda e, tt=tt: e.activation(
                n0[:, ts(tt, C)], x[:, ts(tt, C)], AF.Identity,
                scale=rs4[:, tt:tt + 1], bias=nb4[:, tt:tt + 1]),
                reads=["f32a", "rs4", "nb4"], writes=[K(U_NRM)])
        dump_tm("d_n0", n0, K(U_NRM), ch)
        dump_tm("d_accA", acc, "acc", ch)
        if (DEBUG and (DUMP_CH is None or ch == DUMP_CH)
                and (DUMP_KEYS is None or "d_st" in DUMP_KEYS)):
            sc.op("G", lambda e: e.dma_start(dbg["d_st"][:, 0:8], mv4[:]),
                  reads=["mv4"], dma=True)
            sc.op("G", lambda e: e.dma_start(dbg["d_st"][:, 8:12], rs4[:]),
                  reads=["rs4"], dma=True)
            sc.op("G", lambda e: e.dma_start(dbg["d_st"][:, 12:16], nb4[:]),
                  reads=["nb4"], dma=True)
        ln_stats(n0, K(U_NRM))
        n0n = unit(U_N0N)
        for tt in range(NTT):
            sc.op("S", lambda e, tt=tt: e.activation(
                n0n[:, ts(tt, C)], n0[:, ts(tt, C)], AF.Identity,
                scale=rs4[:, tt:tt + 1], bias=nb4[:, tt:tt + 1]),
                reads=[K(U_NRM), "rs4", "nb4"], writes=[K(U_N0N)])
        h0 = unit(U_H0)
        for tt in range(NTT):
            sc.op("V", lambda e, tt=tt: e.tensor_tensor(
                h0[:, ts(tt, C)], n0n[:, ts(tt, C)], w0bc[:], OP.mult),
                reads=[K(U_N0N), "w0bc"], writes=[K(U_H0)])
            sc.op("V", lambda e, tt=tt: e.tensor_tensor(
                h0[:, ts(tt, C)], h0[:, ts(tt, C)], b0bc[:], OP.add),
                reads=["b0bc"], writes=[K(U_H0)])
        dump_tm("d_h0", h0, K(U_H0), ch)
        hf = unit(U_HF)
        transpose_affine(n0n, K(U_N0N), hf, K(U_HF), VC["ffng"], VC["ffnb"],
                         "vc_ffng", "vc_ffnb")
        dump_fm("d_hf", hf, K(U_HF), ch)
        ln_stats(h0, K(U_H0))
        na1 = unit(U_NRM)
        for tt in range(NTT):
            sc.op("S", lambda e, tt=tt: e.activation(
                na1[:, ts(tt, C)], h0[:, ts(tt, C)], AF.Identity,
                scale=rs4[:, tt:tt + 1], bias=nb4[:, tt:tt + 1]),
                reads=[K(U_H0), "rs4", "nb4"], writes=[K(U_NRM)])
        xa = unit(U_XA)
        transpose_affine(na1, K(U_NRM), xa, K(U_XA), VC["ln1w"], VC["ln1b"],
                         "vc_ln1w", "vc_ln1b")
        dump_fm("d_xa", xa, K(U_XA), ch)
        xx = unit(U_XX)
        xa3 = xa.rearrange("p (ft t) -> p ft t", t=TC)
        xx3 = xx.rearrange("p (ft t) -> p ft t", t=TC)
        sc.op("V", lambda e: e.tensor_tensor(
            xx3[:, :, 1:TC], xa3[:, :, 0:TC - 1], xa3[:, :, 1:TC], OP.subtract),
            reads=[K(U_XA)], writes=[K(U_XX)])
        sc.op("V", lambda e: e.tensor_tensor(
            xx3[:, :, 0:1], xacar[:].unsqueeze(2), xa3[:, :, 0:1], OP.subtract),
            reads=[K(U_XA), "xacar"], writes=[K(U_XX)])
        sc.op("V", lambda e: e.tensor_copy(
            xacar[:].unsqueeze(2), xa3[:, :, TC - 1:TC]),
            reads=[K(U_XA)], writes=["xacar"], lazy=True)
        dump_fm("d_xx", xx, K(U_XX), ch)

        if 'B' not in PHASES:
            return
        t0 = unit(U_XF0)
        t03 = t0.rearrange("p (ft t) -> p ft t", t=TC)
        for ft in range(NFT):
            sc.op("V", lambda e, ft=ft: e.scalar_tensor_tensor(
                t03[:, ft], xx3[:, ft], VC["xmaa"][:, ft:ft + 1], xa3[:, ft],
                OP.mult, OP.add),
                reads=[K(U_XX), K(U_XA), "vc_xmaa"], writes=[K(U_XF0)])
        for mt, (zb, zbuf, mrows) in enumerate(
                [(z0b, "z0", 128), (z1b, "z1", 32)]):
            for kt in range(NFT):
                sc.op("T", lambda e, mt=mt, kt=kt, mrows=mrows: e.matmul(
                    QD0[0:mrows, 0:TC],
                    tmw1[:, kt * TMDIM + mt * 128: kt * TMDIM + mt * 128 + mrows],
                    t03[:, kt], start=(kt == 0), stop=(kt == NFT - 1)),
                    reads=["tmw1", K(U_XF0)], writes=["QD0"])
            sc.op("S", lambda e, zb=zb, mrows=mrows: e.activation(
                zb[:], QD0[0:mrows, 0:TC], AF.Tanh), reads=["QD0"], writes=[zbuf])

        projs = [("w", "wmaa", None), ("k", "kmaa", "Wk"), ("v", "vmaa", "Wv"),
                 ("r", "rmaa", "Wr"), ("g", "gmaa", "Wg")]
        r_fm = unit(U_R); k_fm = unit(U_K); v_tm = unit(U_V); g_tm = unit(U_G)
        ew = unit(U_EW)
        for fi, (fn_, maanm, wnm) in enumerate(projs):
            un = U_XF0 if fi % 2 == 0 else U_XF1
            xf = unit(un)
            xfbuf = K(un)
            xf3 = xf.rearrange("p (ft t) -> p ft t", t=TC)
            zsl = z0b[ts(fi % 4, 32), :] if fi < 4 else z1b[:, :]
            zbuf = "z0" if fi < 4 else "z1"
            zbase = (fi % 4) * 32 if fi < 4 else 0
            w2buf = "w2a" if fi < 4 else "w2b"
            for ft in range(NFT):
                wt = (w2a[ts(fi % 4, 32), ts(ft)] if fi < 4 else w2b[:, ts(ft)])
                QP, qpb = (QD1, "QD1") if ft % 2 == 0 else (QD0, "QD0")
                sfv = mtdd[:, (ft % 2) * 512:(ft % 2) * 512 + 512]
                sc.op("T", lambda e, wt=wt, zsl=zsl, zbase=zbase, QP=QP:
                      e.matmul(
                    QP[:, 0:TC], wt, zsl, start=True, stop=True,
                    tile_position=(zbase, 0)),
                    reads=[w2buf, zbuf], writes=[qpb])
                sc.op("S", lambda e, ft=ft, maanm=maanm, QP=QP, sfv=sfv:
                      e.activation(
                    sfv, QP[:, 0:TC], AF.Identity,
                    bias=VC[maanm][:, ft:ft + 1]),
                    reads=[qpb, "vc_" + maanm], writes=["mtdd"])
                sc.op("V", lambda e, ft=ft, xf3=xf3, sfv=sfv: e.tensor_tensor(
                    xf3[:, ft], xx3[:, ft], sfv, OP.mult),
                    reads=[K(U_XX), "mtdd"], writes=[xfbuf])
                sc.op("V", lambda e, ft=ft, xf3=xf3: e.tensor_tensor(
                    xf3[:, ft], xf3[:, ft], xa3[:, ft], OP.add),
                    reads=[K(U_XA)], writes=[xfbuf])
            if fn_ == "w":
                dump_fm("d_xw", xf, xfbuf, ch)
                for kt in range(NFT):
                    sc.op("T", lambda e, kt=kt, xf3=xf3: e.matmul(
                        QD0[0:64, 0:TC], tdw1[:, kt * TD:(kt + 1) * TD],
                        xf3[:, kt], start=(kt == 0), stop=(kt == NFT - 1)),
                        reads=["tdw1", xfbuf], writes=["QD0"])
                sc.op("S", lambda e: e.activation(t1b[:], QD0[0:64, 0:TC],
                                                  AF.Tanh),
                      reads=["QD0"], writes=["t1b"])
                for ft in range(NFT):
                    sc.op("T", lambda e, ft=ft: e.matmul(
                        QD1[:, 0:TC], tdw2[:, ts(ft)], t1b[:],
                        start=True, stop=True),
                        reads=["tdw2", "t1b"], writes=["QD1"])
                    sc.op("S", lambda e, ft=ft: e.activation(
                        ew[:, ft * TC:(ft + 1) * TC], QD1[:, 0:TC], AF.Exp,
                        bias=VC["tdec"][:, ft:ft + 1]),
                        reads=["QD1", "vc_tdec"], writes=[K(U_EW)])
                dump_fm("d_ew", ew, K(U_EW), ch)
                continue
            if fn_ == "k":
                dump_fm("d_xk", xf, xfbuf, ch)
            wq, wqbuf = piece(ch, wnm)
            if fn_ in ("k", "r"):
                dst = k_fm if fn_ == "k" else r_fm
                dbuf = K(U_K) if fn_ == "k" else K(U_R)
                for mt in range(NFT):
                    QP, qpb = (QD2, "QD2") if mt % 2 == 0 else (QD3, "QD3")
                    for kt in range(NFT):
                        sc.op("T", lambda e, mt=mt, kt=kt, wq=wq, xf3=xf3,
                              QP=QP: e.matmul(
                                  QP[:, 0:TC],
                                  wq[:, kt * C + mt * 128: kt * C + (mt + 1) * 128],
                                  xf3[:, kt], start=(kt == 0),
                                  stop=(kt == NFT - 1)),
                              reads=[wqbuf, xfbuf], writes=[qpb])
                    sc.op("S", lambda e, mt=mt, dst=dst, QP=QP: e.activation(
                        dst[:, mt * TC:(mt + 1) * TC], QP[:, 0:TC],
                        AF.Identity), reads=[qpb], writes=[dbuf])
                dump_fm("d_" + fn_, dst, dbuf, ch)
            else:
                dst = v_tm if fn_ == "v" else g_tm
                dbuf = K(U_V) if fn_ == "v" else K(U_G)
                act_f = AF.Identity if fn_ == "v" else AF.Silu
                for tt in range(NTT):
                    for nh in range(2):
                        QP, qpb = (QD2, "QD2") if (2 * tt + nh) % 2 == 0 \
                            else (QD3, "QD3")
                        for kt in range(NFT):
                            sc.op("T", lambda e, tt=tt, nh=nh, kt=kt, wq=wq,
                                  xf=xf, QP=QP: e.matmul(
                                      QP[:, 0:512],
                                      xf[:, kt * TC + tt * 128:
                                         kt * TC + (tt + 1) * 128],
                                      wq[:, kt * C + nh * 512:
                                         kt * C + (nh + 1) * 512],
                                      start=(kt == 0), stop=(kt == NFT - 1)),
                                  reads=[wqbuf, xfbuf], writes=[qpb])
                        sc.op("S", lambda e, tt=tt, nh=nh, dst=dst,
                              act_f=act_f, QP=QP: e.activation(
                                  dst[:, tt * C + nh * 512:
                                      tt * C + (nh + 1) * 512],
                                  QP[:, 0:512], act_f),
                              reads=[qpb], writes=[dbuf])
                dump_tm("d_" + fn_, dst, dbuf, ch)

        if 'C' not in PHASES:
            return
        csl = f32a[:, 0:1024]
        f1t = f32a[:, 1024:2048]
        rbb = f32a[:, 2048:3072]
        mrb = f32a[:, 3072:4096]
        pk1 = unit(U_N0N)
        F1 = pk1[:, 0:1024]; F2 = pk1[:, 1024:2048]
        Kh = pk1[:, 2048:3072]; uk = pk1[:, 3072:4096]
        pk2 = unit(U_XF0)
        Rt = pk2[:, 0:1024]; ktm = pk2[:, 2048:3072]; y3l = pk2[:, 3072:4096]
        y3 = unit(U_XA)
        ew3 = ew.rearrange("p (ft t) -> p ft t", t=TC)
        k3 = k_fm.rearrange("p (ft t) -> p ft t", t=TC)
        r3 = r_fm.rearrange("p (ft t) -> p ft t", t=TC)
        cs3 = csl.rearrange("p (ft t) -> p ft t", t=128)
        f1t3 = f1t.rearrange("p (ft t) -> p ft t", t=128)
        F13 = F1.rearrange("p (ft t) -> p ft t", t=128)
        F23 = F2.rearrange("p (ft t) -> p ft t", t=128)
        kh3 = Kh.rearrange("p (ft t) -> p ft t", t=128)
        uk3 = uk.rearrange("p (ft t) -> p ft t", t=128)
        Rt3 = Rt.rearrange("p (ft t) -> p ft t", t=128)
        for l in range(NL):
            YP, ypbuf = YY, "YY"
            for ft in range(NFT):
                sc.op("V", lambda e, ft=ft, l=l: e.tensor_tensor_scan(
                    cs3[:, ft], ew3[:, ft, l * 128:(l + 1) * 128],
                    ones128[:], 0.0, OP.add, OP.mult),
                    reads=[K(U_EW), "ones", "f32a"], writes=["f32cs"],
                    lazy=True)
            sc.op("V", lambda e: e.tensor_copy(
                AL8[:].unsqueeze(2), cs3[:, :, 127:128]),
                reads=["f32cs"], writes=["AL8"], lazy=True)
            sc.op("S", lambda e: e.activation(AL8[:], AL8[:], AF.Exp,
                                              scale=-1.0),
                reads=["AL8"], writes=["AL8"])
            sc.op("V", lambda e, l=l: e.tensor_tensor(
                f1t3[:], ew3[:, :, l * 128:(l + 1) * 128], cs3[:],
                OP.subtract), reads=[K(U_EW), "f32cs"], writes=["f32cs"])
            sc.op("S", lambda e: e.activation(F1[:], f1t, AF.Exp),
                  reads=["f32cs"], writes=[K(U_N0N)])
            sc.op("S", lambda e: e.activation(F2[:], csl, AF.Exp),
                  reads=["f32cs"], writes=[K(U_N0N)])
            sc.op("V", lambda e: e.tensor_tensor(
                kh3[:], F23[:],
                AL8[:].unsqueeze(2).broadcast_to([128, NFT, 128]), OP.mult),
                reads=[K(U_N0N), "AL8"], writes=[K(U_N0N)])
            sc.op("V", lambda e, l=l: e.tensor_tensor(
                kh3[:], kh3[:], k3[:, :, l * 128:(l + 1) * 128], OP.mult),
                reads=[K(U_K)], writes=[K(U_N0N)])
            sc.op("V", lambda e, l=l: e.tensor_tensor(
                F23[:], F23[:], k3[:, :, l * 128:(l + 1) * 128], OP.mult),
                reads=[K(U_K)], writes=[K(U_N0N)])
            sc.op("V", lambda e, l=l: e.tensor_tensor(
                Rt3[:], F13[:], r3[:, :, l * 128:(l + 1) * 128], OP.mult),
                reads=[K(U_R), K(U_N0N)], writes=[K(U_XF0)])
            sc.op("V", lambda e, l=l: e.tensor_tensor(
                uk3[:], k3[:, :, l * 128:(l + 1) * 128],
                VC["ucol"][:].unsqueeze(2).broadcast_to([128, NFT, 128]),
                OP.mult), reads=[K(U_K), "vc_ucol"], writes=[K(U_N0N)])
            if CLEV < 2:
                continue
            for hp in range(NFT):
                sc.op("T", lambda e, hp=hp: e.transpose(
                    PST[:, ts(hp)], Kh[:, ts(hp)], ident[:]),
                    reads=[K(U_N0N), "ident"], writes=["PST"])
            sc.op("S", lambda e: e.activation(ktm[:], PST[:], AF.Identity),
                  reads=["PST"], writes=[K(U_XF0)])
            if CLEV < 3:
                continue
            QDS = [QD0, QD1, QD2, QD3]
            QBUF = ["QD0", "QD1", "QD2", "QD3"]

            def qtd_round(r):
                for par, b0 in ((0, 0), (64, 2)):
                    for i in range(4):
                        hp = 4 * r + i
                        bk = b0 + i // 2
                        off = (i % 2) * 256
                        sc.op("T", lambda e, hp=hp, par=par, bk=bk, off=off:
                              e.matmul(
                                  QDS[bk][:, off:off + 128],
                                  F2[par:par + 64, ts(hp)],
                                  Rt[par:par + 64, ts(hp)],
                                  start=True, stop=True,
                                  tile_position=(par, 0)),
                              reads=[K(U_XF0), K(U_N0N)], writes=[QBUF[bk]])
                        sc.op("T", lambda e, hp=hp, par=par, bk=bk, off=off,
                              l=l: e.matmul(
                                  QDS[bk][:, off + 128:off + 256],
                                  uk[par:par + 64, ts(hp)],
                                  r_fm[par:par + 64, hp * TC + l * 128:
                                       hp * TC + (l + 1) * 128],
                                  start=True, stop=True,
                                  tile_position=(par, 0)),
                              reads=[K(U_N0N), K(U_R)], writes=[QBUF[bk]])

            def masks_round(r):
                for bk in range(4):
                    sc.op("V", lambda e, bk=bk: e.tensor_tensor(
                        mtdd[:, bk * 512:(bk + 1) * 512], QDS[bk][:, 0:512],
                        maskQ[:], OP.mult),
                        reads=[QBUF[bk], "maskQ"], writes=["mtdd"])

            def y_round(r):
                for i in range(4):
                    hp = 4 * r + i
                    for hsub in range(2):
                        h = hp * 2 + hsub
                        par = hsub * 64
                        base = ((i // 2) + 2 * hsub) * 512 + (i % 2) * 256
                        vsl = v_tm[:, l * C + h * HN: l * C + (h + 1) * HN]
                        ysl = YP[:, h * HN:(h + 1) * HN]
                        sc.op("T", lambda e, base=base, vsl=vsl, ysl=ysl:
                              e.matmul(
                                  ysl, mtdd[:, base:base + 128], vsl,
                                  start=True, stop=False),
                              reads=["mtdd", K(U_V)], writes=[ypbuf])
                        sc.op("T", lambda e, base=base, vsl=vsl, ysl=ysl:
                              e.matmul(
                                  ysl, mtdd[:, base + 128:base + 256], vsl,
                                  start=False, stop=False),
                              reads=["mtdd", K(U_V)], writes=[ypbuf])
                        sc.op("T", lambda e, hp=hp, par=par, ysl=ysl: e.matmul(
                            ysl, Rt[par:par + 64, ts(hp)],
                            S_b[par:par + 64, hp * HN:(hp + 1) * HN],
                            start=False, stop=True, tile_position=(par, 0)),
                            reads=[K(U_XF0), "S_b"], writes=[ypbuf])
                        sc.op("T", lambda e, hp=hp, par=par, vsl=vsl:
                              e.matmul(
                            SS[par:par + 64, hp * HN:(hp + 1) * HN],
                            ktm[:, hp * 128 + par: hp * 128 + par + 64], vsl,
                            start=True, stop=True, tile_position=(0, par)),
                            reads=[K(U_XF0), K(U_V)], writes=["SS"])

            qtd_round(0)
            masks_round(0)
            qtd_round(1)
            if CLEV >= 4:
                y_round(0)
            masks_round(1)
            if CLEV >= 4:
                y_round(1)
            if CLEV < 5:
                continue
            for hp in range(NFT):
                sc.op("V", lambda e, hp=hp: e.scalar_tensor_tensor(
                    S_f[:, hp * HN:(hp + 1) * HN], S_f[:, hp * HN:(hp + 1) * HN],
                    AL8[:, hp:hp + 1], SS[:, hp * HN:(hp + 1) * HN],
                    OP.mult, OP.add), reads=["AL8", "SS"], writes=["S_f"])
            sc.op("S", lambda e: e.activation(S_b[:], S_f[:], AF.Identity),
                  reads=["S_f"], writes=["S_b"])
            if CLEV < 6:
                continue
            sc.op("V", lambda e, YP=YP: e.tensor_reduce(
                gn16[:, 0:16], YP[:].rearrange("p (h n) -> p h n", h=H),
                axis=AX.X, op=OP.add), reads=[ypbuf], writes=["gn16"],
                lazy=True)
            sc.op("S", lambda e, YP=YP: e.activation(ysq[:], YP[:], AF.Square),
                  reads=[ypbuf], writes=["mtdd"])
            sc.op("V", lambda e: e.tensor_reduce(
                gn16[:, 16:32], ysq[:].rearrange("p (h n) -> p h n", h=H),
                axis=AX.X, op=OP.add), reads=["mtdd"], writes=["gn16"],
                lazy=True)
            sc.op("V", lambda e: e.tensor_tensor(
                gn16[:, 32:48], gn16[:, 0:16], gn16[:, 0:16], OP.mult),
                reads=["gn16"], writes=["gn16"])
            sc.op("V", lambda e: e.scalar_tensor_tensor(
                gn16[:, 16:32], gn16[:, 16:32], float(HN), gn16[:, 32:48],
                OP.mult, OP.subtract),
                reads=["gn16"], writes=["gn16"])
            sc.op("S", lambda e: e.activation(
                gn16[:, 32:48], gn16[:, 16:32], AF.Ln, bias=epsgn[:, 0:1]),
                reads=["gn16", "epsgn"], writes=["gn16"])
            sc.op("S", lambda e: e.activation(
                gn16[:, 32:48], gn16[:, 32:48], AF.Exp, scale=-0.5,
                bias=b4k[:, 0:1]),
                reads=["gn16", "b4k"], writes=["gn16"])
            sc.op("V", lambda e: e.scalar_tensor_tensor(
                gn16[:, 48:64], gn16[:, 0:16], 1.0 / HN, gn16[:, 32:48],
                OP.mult, OP.mult),
                reads=["gn16"], writes=["gn16"])
            sc.op("V", lambda e: e.tensor_copy(
                rbb.rearrange("p (h n) -> p h n", h=H),
                gn16[:, 32:48].unsqueeze(2).broadcast_to([128, H, HN])),
                reads=["gn16", "f32a"], writes=["f32gn"], lazy=True)
            sc.op("V", lambda e: e.tensor_copy(
                mrb.rearrange("p (h n) -> p h n", h=H),
                gn16[:, 48:64].unsqueeze(2).broadcast_to([128, H, HN])),
                reads=["gn16", "f32a"], writes=["f32gn"], lazy=True)
            sc.op("V", lambda e, YP=YP: e.tensor_tensor(
                y3l[:], YP[:], rbb, OP.mult),
                reads=[ypbuf, "f32gn"], writes=[K(U_XF0)])
            sc.op("V", lambda e: e.tensor_tensor(y3l[:], y3l[:], mrb,
                                                 OP.subtract),
                  reads=["f32gn"], writes=[K(U_XF0)])
            sc.op("V", lambda e: e.tensor_tensor(y3l[:], y3l[:], lnxwbc[:],
                                                 OP.mult),
                  reads=["lnxw"], writes=[K(U_XF0)])
            sc.op("V", lambda e: e.tensor_tensor(y3l[:], y3l[:], lnxbbc[:],
                                                 OP.add),
                  reads=["lnxb"], writes=[K(U_XF0)])
            sc.op("V", lambda e, l=l: e.tensor_tensor(
                y3l[:], y3l[:], g_tm[:, l * C:(l + 1) * C], OP.mult),
                reads=[K(U_G)], writes=[K(U_XF0)])
            if (DEBUG and l == 0 and (DUMP_CH is None or ch == DUMP_CH)
                    and (DUMP_KEYS is None or "d_cs" in DUMP_KEYS)):
                for nm2, src2, buf2 in [
                        ("d_cs", csl, "f32a"), ("d_F1", F1, K(U_N0N)),
                        ("d_F2", F2, K(U_N0N)), ("d_Kh", Kh, K(U_N0N)),
                        ("d_Rt", Rt, K(U_XF0)), ("d_ktm", ktm, K(U_XF0)),
                        ("d_mt", mtdd[:, 0:2048], "mtdd"),
                        ("d_Sf", S_f[:], "S_f"), ("d_gn", gn16[:], "gn16"),
                        ("d_y3l", y3l, K(U_XF0)), ("d_ysq", ysq, "mtdd")]:
                    sc.op("G", lambda e, nm2=nm2, src2=src2: e.dma_start(
                        dbg[nm2], src2), reads=[buf2], dma=True)
            q = l
            wq1, wq1buf = piece(ch, f"fc1{q}")
            hf3i = hf.rearrange("p (ft t) -> p ft t", t=TC)
            for mt in range(NFT):
                gmt = q * NFT + mt
                QP, qpb = (QD2, "QD2") if gmt % 2 == 0 else (QD3, "QD3")
                ysl1, ykey = y1sl(gmt)
                for kt in range(NFT):
                    sc.op("T", lambda e, mt=mt, kt=kt, wq1=wq1, QP=QP,
                          hf3i=hf3i: e.matmul(
                        QP[:, 0:TC],
                        wq1[:, kt * C + mt * 128: kt * C + (mt + 1) * 128],
                        hf3i[:, kt], start=(kt == 0), stop=(kt == NFT - 1)),
                        reads=[wq1buf, K(U_HF)], writes=[qpb])
                sc.op("S", lambda e, gmt=gmt, QP=QP, ysl1=ysl1: e.activation(
                    ysl1, QP[:, 0:TC], AF.Identity,
                    bias=VC["fc1b"][:, gmt:gmt + 1]),
                    reads=[qpb, "vc_fc1b"], writes=[ykey])
            for ft in range(NFT):
                sc.op("T", lambda e, ft=ft: e.transpose(
                    PST[:, ts(ft)], y3l[:, ts(ft)], ident[:]),
                    reads=[K(U_XF0), "ident"], writes=["PST"])
            dst = y3.rearrange("p (ft t) -> p ft t", t=TC)[:, :,
                                                           l * 128:(l + 1) * 128]
            sc.op("S", lambda e, dst=dst: e.activation(
                dst, PST[:].rearrange("p (ft t) -> p ft t", t=128),
                AF.Identity), reads=["PST"], writes=[K(U_XA)])
        dump_fm("d_y3", y3, K(U_XA), ch)

        if 'E' not in PHASES:
            return
        att = unit(U_EW)
        wq, wqbuf = piece(ch, "Wo")
        for tt in range(NTT):
            for nh in range(2):
                QP, qpb = (QD2, "QD2") if (2 * tt + nh) % 2 == 0 \
                    else (QD3, "QD3")
                for kt in range(NFT):
                    sc.op("T", lambda e, tt=tt, nh=nh, kt=kt, wq=wq, QP=QP:
                          e.matmul(
                        QP[:, 0:512],
                        y3[:, kt * TC + tt * 128: kt * TC + (tt + 1) * 128],
                        wq[:, kt * C + nh * 512: kt * C + (nh + 1) * 512],
                        start=(kt == 0), stop=(kt == NFT - 1)),
                        reads=[wqbuf, K(U_XA)], writes=[qpb])
                asl = att[:, tt * C + nh * 512: tt * C + (nh + 1) * 512]
                hsl = h0[:, tt * C + nh * 512: tt * C + (nh + 1) * 512]
                accsl = acc[:, tt * C + nh * 512: tt * C + (nh + 1) * 512]
                sc.op("V", lambda e, asl=asl, hsl=hsl, QP=QP: e.tensor_tensor(
                    asl, QP[:, 0:512], hsl, OP.add),
                    reads=[qpb, K(U_H0)], writes=[K(U_EW)])
                sc.op("V", lambda e, accsl=accsl, asl=asl: e.tensor_tensor(
                    accsl, accsl, asl, OP.add),
                    reads=[K(U_EW)], writes=["acc"])
        dump_tm("d_att", att, K(U_EW), ch)

        for gkt in range(32):
            ysl1, ykey = y1sl(gkt)
            sc.op("S", lambda e, ysl1=ysl1: e.activation(ysl1, ysl1, AF.Silu),
                  reads=[ykey], writes=[ykey])
        for half in range(2):
            slices2 = [(tt, nh) for tt in (2 * half, 2 * half + 1)
                       for nh in range(2)]
            for q in range(4):
                wq2, wq2buf = piece(ch, f"fc2{q}h{half}")
                for si, (tt, nh) in enumerate(slices2):
                    ZP, zpbuf, zoff = [(YY, "YY", 0), (YY, "YY", 512),
                                       (QD0, "QD0", 0), (QD1, "QD1", 0)][si]
                    for kt in range(NFT):
                        gkt = q * NFT + kt
                        ysl1, ykey = y1sl(gkt, tt)
                        sc.op("T", lambda e, kt=kt, wq2=wq2, ZP=ZP, zoff=zoff,
                              ysl1=ysl1, nh=nh, q=q: e.matmul(
                                  ZP[:, zoff:zoff + 512], ysl1,
                                  wq2[:, kt * C + nh * 512:
                                      kt * C + (nh + 1) * 512],
                                  start=(q == 0 and kt == 0),
                                  stop=(q == 3 and kt == NFT - 1)),
                              reads=[wq2buf, ykey], writes=[zpbuf])
            for si, (tt, nh) in enumerate(slices2):
                ZP, zpbuf, zoff = [(YY, "YY", 0), (YY, "YY", 512),
                                   (QD0, "QD0", 0), (QD1, "QD1", 0)][si]
                accsl = acc[:, tt * C + nh * 512: tt * C + (nh + 1) * 512]
                sc.op("V", lambda e, accsl=accsl, ZP=ZP, zoff=zoff:
                      e.scalar_tensor_tensor(
                          accsl, ZP[:, zoff:zoff + 512], 0.5, accsl,
                          OP.mult, OP.add),
                      reads=[zpbuf], writes=["acc"])
        fc2bb = mtdd[:, 1024:2048]
        sc.op("G", lambda e: e.dma_start(fc2bb, cfc2bb),
              writes=["mtdd"], dma=True)
        for tt in range(NTT):
            sc.op("V", lambda e, tt=tt, fc2bb=fc2bb: e.scalar_tensor_tensor(
                acc[:, ts(tt, C)], fc2bb, 0.5, acc[:, ts(tt, C)],
                OP.mult, OP.add), reads=["mtdd"], writes=["acc"])

        if 'F' not in PHASES:
            return
        ln_stats(att, K(U_EW))
        nc2 = unit(U_NRM)
        for tt in range(NTT):
            sc.op("S", lambda e, tt=tt: e.activation(
                nc2[:, ts(tt, C)], att[:, ts(tt, C)], AF.Identity,
                scale=rs4[:, tt:tt + 1], bias=nb4[:, tt:tt + 1]),
                reads=[K(U_EW), "rs4", "nb4"], writes=[K(U_NRM)])
        xc = unit(U_XA)
        transpose_affine(nc2, K(U_NRM), xc, K(U_XA), VC["ln2w"], VC["ln2b"],
                         "vc_ln2w", "vc_ln2b")
        dump_fm("d_xc", xc, K(U_XA), ch)
        cxx = unit(U_XX)
        xc3 = xc.rearrange("p (ft t) -> p ft t", t=TC)
        cx3 = cxx.rearrange("p (ft t) -> p ft t", t=TC)
        sc.op("V", lambda e: e.tensor_tensor(
            cx3[:, :, 1:TC], xc3[:, :, 0:TC - 1], xc3[:, :, 1:TC], OP.subtract),
            reads=[K(U_XA)], writes=[K(U_XX)])
        sc.op("V", lambda e: e.tensor_tensor(
            cx3[:, :, 0:1], xccar[:].unsqueeze(2), xc3[:, :, 0:1], OP.subtract),
            reads=[K(U_XA), "xccar"], writes=[K(U_XX)])
        sc.op("V", lambda e: e.tensor_copy(
            xccar[:].unsqueeze(2), xc3[:, :, TC - 1:TC]),
            reads=[K(U_XA)], writes=["xccar"], lazy=True)
        kx = unit(U_XF0)
        rx = unit(U_XF1)
        kx3 = kx.rearrange("p (ft t) -> p ft t", t=TC)
        rx3 = rx.rearrange("p (ft t) -> p ft t", t=TC)
        for ft in range(NFT):
            sc.op("V", lambda e, ft=ft: e.scalar_tensor_tensor(
                kx3[:, ft], cx3[:, ft], VC["fkmaa"][:, ft:ft + 1], xc3[:, ft],
                OP.mult, OP.add),
                reads=[K(U_XX), K(U_XA), "vc_fkmaa"], writes=[K(U_XF0)])
            sc.op("V", lambda e, ft=ft: e.scalar_tensor_tensor(
                rx3[:, ft], cx3[:, ft], VC["frmaa"][:, ft:ft + 1], xc3[:, ft],
                OP.mult, OP.add),
                reads=[K(U_XX), K(U_XA), "vc_frmaa"], writes=[K(U_XF1)])
        dump_fm("d_kx", kx, K(U_XF0), ch)

        if 'G' not in PHASES:
            return
        sig = unit(U_XG)
        wq, wqbuf = piece(ch, "Wrf")
        for tt in range(NTT):
            for nh in range(2):
                QP, qpb = (QD2, "QD2") if (2 * tt + nh) % 2 == 0 \
                    else (QD3, "QD3")
                for kt in range(NFT):
                    sc.op("T", lambda e, tt=tt, nh=nh, kt=kt, wq=wq, QP=QP:
                          e.matmul(
                        QP[:, 0:512],
                        rx[:, kt * TC + tt * 128: kt * TC + (tt + 1) * 128],
                        wq[:, kt * C + nh * 512: kt * C + (nh + 1) * 512],
                        start=(kt == 0), stop=(kt == NFT - 1)),
                        reads=[wqbuf, K(U_XF1)], writes=[qpb])
                sc.op("S", lambda e, tt=tt, nh=nh, QP=QP: e.activation(
                    sig[:, tt * C + nh * 512: tt * C + (nh + 1) * 512],
                    QP[:, 0:512], AF.Sigmoid), reads=[qpb], writes=[K(U_XG)])
        dump_tm("d_sig", sig, K(U_XG), ch)

        if 'H' not in PHASES:
            return
        kkfull = bfa[:, U_R * U: U_R * U + FFN * 4]
        kkkeys = [K(U_R), K(U_K), K(U_V), K(U_G)]
        for q in range(4):
            wq, wqbuf = piece(ch, f"wkf{q}")
            for mt in range(7):
                gmt = q * 7 + mt
                QP, qpb = (QD2, "QD2") if gmt % 2 == 0 else (QD3, "QD3")
                for kt in range(NFT):
                    sc.op("T", lambda e, mt=mt, kt=kt, wq=wq, QP=QP: e.matmul(
                        QP[:, 0:TC],
                        wq[:, kt * 896 + mt * 128: kt * 896 + (mt + 1) * 128],
                        kx3[:, kt], start=(kt == 0), stop=(kt == NFT - 1)),
                        reads=[wqbuf, K(U_XF0)], writes=[qpb])
                sc.op("S", lambda e, gmt=gmt, QP=QP: e.activation(
                    kkfull[:, gmt * TC:(gmt + 1) * TC], QP[:, 0:TC], AF.Relu),
                    reads=[qpb], writes=kkkeys)
                sc.op("V", lambda e, gmt=gmt: e.tensor_tensor(
                    kkfull[:, gmt * TC:(gmt + 1) * TC],
                    kkfull[:, gmt * TC:(gmt + 1) * TC],
                    kkfull[:, gmt * TC:(gmt + 1) * TC], OP.mult),
                    reads=kkkeys, writes=kkkeys)
        dump_fm("d_kk", kkfull, kkkeys[0], ch)
        d2 = unit(U_XX)
        for half in range(2):
            slices = [(tt, nh) for tt in (2 * half, 2 * half + 1)
                      for nh in range(2)]
            for q in range(4):
                wq, wqbuf = piece(ch, f"wvf{q}h{half}")
                for si, (tt, nh) in enumerate(slices):
                    ZP, zpbuf, zoff = [(YY, "YY", 0), (YY, "YY", 512),
                                       (QD0, "QD0", 0), (QD1, "QD1", 0)][si]
                    for kt in range(7):
                        gkt = q * 7 + kt
                        sc.op("T", lambda e, tt=tt, nh=nh, kt=kt, gkt=gkt,
                              wq=wq, q=q, ZP=ZP, zoff=zoff: e.matmul(
                                  ZP[:, zoff:zoff + 512],
                                  kkfull[:, gkt * TC + tt * 128:
                                         gkt * TC + (tt + 1) * 128],
                                  wq[:, kt * C + nh * 512:
                                     kt * C + (nh + 1) * 512],
                                  start=(q == 0 and kt == 0),
                                  stop=(q == 3 and kt == 6)),
                              reads=[wqbuf] + kkkeys, writes=[zpbuf])
            for si, (tt, nh) in enumerate(slices):
                ZP, zpbuf, zoff = [(YY, "YY", 0), (YY, "YY", 512),
                                   (QD0, "QD0", 0), (QD1, "QD1", 0)][si]
                sgl = sig[:, tt * C + nh * 512: tt * C + (nh + 1) * 512]
                dsl = d2[:, tt * C + nh * 512: tt * C + (nh + 1) * 512]
                accsl = acc[:, tt * C + nh * 512: tt * C + (nh + 1) * 512]
                sc.op("V", lambda e, sgl=sgl, dsl=dsl, ZP=ZP, zoff=zoff:
                      e.tensor_tensor(dsl, ZP[:, zoff:zoff + 512], sgl,
                                      OP.mult),
                      reads=[zpbuf, K(U_XG)], writes=[K(U_XX)])
                sc.op("V", lambda e, accsl=accsl, dsl=dsl: e.tensor_tensor(
                    accsl, accsl, dsl, OP.add), reads=[K(U_XX)], writes=["acc"])

        dump_tm("d_d2", d2, K(U_XX), ch)
        dump_tm("d_accH", acc, "acc", ch)
        if 'I' not in PHASES:
            return
        sc.op("Y", lambda e, t0g=t0g: e.dma_start(
            y_out[t0g:t0g + TC, :].rearrange("(tt p) c -> p tt c", p=128),
            acc[:].rearrange("p (tt c) -> p tt c", tt=NTT)),
            reads=["acc"], writes=["yout"], dma=True)

    prologue()
    for ch in range(NCH):
        chunk(ch)
    total_y = {f"Y{i}": sc.cnt[f"Y{i}"] for i in range(8)}

    streams = {e: [] for e in ("T", "V", "S", "G", "Y")}
    for eng, fn, waits, inc, chan in sc.ops:
        streams[eng].append((fn, waits, inc, chan))
    watermark = {e: {} for e in ("T", "V", "S", "G", "Y")}

    def emit_stream(engobj, ename):
        wm = watermark[ename]
        for fn, waits, inc, chan in streams[ename]:
            for (e2, c2) in waits:
                if wm.get(e2, 0) >= c2:
                    continue
                engobj.wait_ge(sems[e2], c2)
                wm[e2] = c2
            inst = fn(engobj)
            inst.then_inc(sems[chan], inc)

    block = ec(nc.Block())

    @block.tensor
    def _(eng):
        emit_stream(eng, "T")

    @block.vector
    def _(eng):
        emit_stream(eng, "V")

    @block.scalar
    def _(eng):
        emit_stream(eng, "S")

    @block.gpsimd
    def _(eng):
        emit_stream(eng, "G")

    @block.sync
    def _(eng):
        emit_stream(eng, "Y")
        for i in range(8):
            if total_y[f"Y{i}"]:
                eng.wait_ge(sems[f"Y{i}"], total_y[f"Y{i}"])

    es.close()
    return nc


_NC_CACHE = None


def kernel(**inputs):
    global _NC_CACHE
    import ml_dtypes
    x = np.asarray(inputs["L0_attn_x"], np.float32)
    B = x.shape[0]

    def bf(a):
        return np.ascontiguousarray(a).astype(ml_dtypes.bfloat16)

    maskU1 = np.triu(np.ones((128, 128), np.float32), 1)
    eye = np.eye(128, dtype=np.float32)

    def tile(v):
        return np.ascontiguousarray(
            np.broadcast_to(np.asarray(v, np.float32), (128, C)))

    common = {
        "ln0_w": inputs["L0_attn_ln0_w"], "ln0_b": inputs["L0_attn_ln0_b"],
        "ln1_w": inputs["L0_attn_ln1_w"], "ln1_b": inputs["L0_attn_ln1_b"],
        "ln2_w": inputs["L0_attn_ln2_w"], "ln2_b": inputs["L0_attn_ln2_b"],
        "x_maa": inputs["L0_attn_x_maa"], "w_maa": inputs["L0_attn_w_maa"],
        "k_maa": inputs["L0_attn_k_maa"], "v_maa": inputs["L0_attn_v_maa"],
        "r_maa": inputs["L0_attn_r_maa"], "g_maa": inputs["L0_attn_g_maa"],
        "tm_w1": inputs["L0_attn_tm_w1"], "tm_w2": inputs["L0_attn_tm_w2"],
        "td_w1": inputs["L0_attn_td_w1"], "td_w2": inputs["L0_attn_td_w2"],
        "time_decay": inputs["L0_attn_time_decay"],
        "time_faaaa": inputs["L0_attn_time_faaaa"],
        "Wr": inputs["L0_attn_Wr"], "Wk": inputs["L0_attn_Wk"],
        "Wv": inputs["L0_attn_Wv"], "Wg": inputs["L0_attn_Wg"],
        "Wo": inputs["L0_attn_Wo"],
        "lnx_w": inputs["L0_attn_lnx_w"], "lnx_b": inputs["L0_attn_lnx_b"],
        "fk_maa": inputs["L0_attn_fk_maa"], "fr_maa": inputs["L0_attn_fr_maa"],
        "Wk_f": inputs["L0_attn_Wk_f"], "Wv_f": inputs["L0_attn_Wv_f"],
        "Wr_f": inputs["L0_attn_Wr_f"],
        "ffn_g": inputs["L0_ffn_ln_gamma"], "ffn_b": inputs["L0_ffn_ln_beta"],
        "fc1_q": inputs["L0_ffn_fc1_q"], "fc1_bias": inputs["L0_ffn_fc1_bias"],
        "fc2_q": inputs["L0_ffn_fc2_q"], "fc2_bias": inputs["L0_ffn_fc2_bias"],
        "cmaskQ": bf(np.concatenate([maskU1, eye, maskU1, eye], 1)),
        "cident": bf(eye),
        "cw0bc": tile(inputs["L0_attn_ln0_w"]),
        "cb0bc": tile(inputs["L0_attn_ln0_b"]),
        "clnxw": tile(inputs["L0_attn_lnx_w"]),
        "clnxb": tile(inputs["L0_attn_lnx_b"]),
        "cfc2bb": tile(inputs["L0_ffn_fc2_bias"]),
        "cfc1s": np.ascontiguousarray(
            np.repeat(np.asarray(inputs["L0_ffn_fc1_scales"], np.float32),
                      128, axis=0)),
        "cfc2s": np.ascontiguousarray(
            np.repeat(np.asarray(inputs["L0_ffn_fc2_scales"], np.float32),
                      128, axis=0)),
    }
    common = {k2: np.ascontiguousarray(np.asarray(v)) for k2, v in common.items()}

    if _NC_CACHE is None:
        _NC_CACHE = build_nc()
    nc = _NC_CACHE
    in_maps = []
    for b in range(B):
        m = dict(common)
        m["x"] = np.ascontiguousarray(x[b])
        in_maps.append(m)
    res = run_bass_kernel_spmd(nc, in_maps, core_ids=list(range(B)))
    out = np.stack([np.asarray(res.results[b]["y"], np.float32)
                    for b in range(B)], axis=0)
    kernel.results = res.results
    return out


# version 13
# speedup vs baseline: 1.0250x; 1.0250x over previous
import numpy as np
import contextlib
import concourse.bass as bass
import concourse.mybir as mybir
from concourse.bass_utils import run_bass_kernel_spmd

F32 = mybir.dt.float32
BF16 = mybir.dt.bfloat16
I32 = mybir.dt.int32
AF = mybir.ActivationFunctionType
OP = mybir.AluOpType
AX = mybir.AxisListType

T_FULL = 4096
C = 1024
TC = 512
NCH = T_FULL // TC
L = 128
NL = TC // L
NTT = TC // 128
NFT = C // 128
H = 16
HN = 64
FFN = 3584
FH = 4096
TD = 64
TMDIM = 160
LN_EPS = 1e-5
GN_EPS = 0.00064

DEBUG = False
DUMP_CH = None
DUMP_KEYS = None
LAZY_ENGINES = ('V',)
PHASES = 'ABCDEFGHI'
CLEV = 9


class Sched:
    def __init__(self):
        self.ops = []
        self.cnt = {e: 0 for e in
                    ["T", "V", "S", "G"]
                    + [f"Y{i}" for i in range(8)] + [f"D{i}" for i in range(8)]}
        self.dmarr = {"Y": -1, "D": -1}
        self.writer = {}
        self.readers = {}
        self.pending_barrier = None
        self.barrier_seen = set()

    def op(self, eng, fn, reads=(), writes=(), dma=False, lazy=False):
        if dma:
            q = "D" if eng == "G" else "Y"
            self.dmarr[q] = (self.dmarr[q] + 1) % 8
            chan = f"{q}{self.dmarr[q]}"
        else:
            chan = eng
        waits = {}

        def add(ev):
            e2, c2, isdma = ev
            if e2 == chan and not isdma:
                return
            waits[e2] = max(waits.get(e2, 0), c2)

        for r in reads:
            w = self.writer.get(r)
            if w is not None:
                add(w)
        for wb in writes:
            w = self.writer.get(wb)
            if w is not None:
                add(w)
            for ev in self.readers.get(wb, ()):
                add(ev)
        if dma and self.cnt[chan] > 0:
            waits[chan] = max(waits.get(chan, 0), self.cnt[chan])
        inc = 16 if dma else 1
        self.cnt[chan] += inc
        ev = (chan, self.cnt[chan], dma or lazy or (eng in LAZY_ENGINES))
        for r in reads:
            self.readers.setdefault(r, []).append(ev)
        for wb in writes:
            self.writer[wb] = ev
            self.readers[wb] = []
        if self.pending_barrier:
            for e2, c2 in self.pending_barrier.items():
                if e2 == chan and not dma:
                    continue
                if c2 > 0:
                    waits[e2] = max(waits.get(e2, 0), c2)
            if eng in self.barrier_seen:
                pass
            self.barrier_seen.add(eng)
        self.ops.append((eng, fn, sorted(waits.items()), inc, chan))

    def barrier(self):
        self.pending_barrier = dict(self.cnt)
        self.barrier_seen = set()


def ts(i, n=128):
    return slice(i * n, (i + 1) * n)


def build_nc():
    nc = bass.Bass("TRN2", target_bir_lowering=False, debug=False,
                   detect_race_conditions=False)

    def din(name, shape, dt=F32):
        return nc.dram_tensor(name, shape, dt, kind="ExternalInput").ap()

    x_in = din("x", [T_FULL, C])
    ln0_w = din("ln0_w", [C]); ln0_b = din("ln0_b", [C])
    ln1_w = din("ln1_w", [C]); ln1_b = din("ln1_b", [C])
    ln2_w = din("ln2_w", [C]); ln2_b = din("ln2_b", [C])
    maa = {k: din(k + "_maa", [C]) for k in ("x", "w", "k", "v", "r", "g")}
    tm_w1 = din("tm_w1", [C, TMDIM])
    tm_w2 = din("tm_w2", [5, 32, C])
    td_w1 = din("td_w1", [C, TD])
    td_w2 = din("td_w2", [TD, C])
    time_decay = din("time_decay", [C])
    time_faaaa = din("time_faaaa", [H, HN])
    Wr = din("Wr", [C, C]); Wk = din("Wk", [C, C]); Wv = din("Wv", [C, C])
    Wg = din("Wg", [C, C]); Wo = din("Wo", [C, C])
    lnx_w = din("lnx_w", [C]); lnx_b = din("lnx_b", [C])
    fk_maa = din("fk_maa", [C]); fr_maa = din("fr_maa", [C])
    Wk_f = din("Wk_f", [C, FFN]); Wv_f = din("Wv_f", [FFN, C])
    Wr_f = din("Wr_f", [C, C])
    ffn_g = din("ffn_g", [C]); ffn_b = din("ffn_b", [C])
    fc1_q = din("fc1_q", [C, FH], I32)
    fc1_bias = din("fc1_bias", [FH])
    fc2_q = din("fc2_q", [FH, C], I32)
    fc2_bias = din("fc2_bias", [C])
    cmaskQ = din("cmaskQ", [128, 512], BF16)
    cident = din("cident", [128, 128], BF16)
    cw0bc = din("cw0bc", [128, C]); cb0bc = din("cb0bc", [128, C])
    clnxw = din("clnxw", [128, C]); clnxb = din("clnxb", [128, C])
    cfc2bb = din("cfc2bb", [128, C])
    cfc1s = din("cfc1s", [C, FH])
    cfc2s = din("cfc2s", [FH, C])

    s_fc1 = nc.dram_tensor("s_fc1", [C, FH], BF16, kind="Internal").ap()
    s_fc2 = nc.dram_tensor("s_fc2", [FH, C], BF16, kind="Internal").ap()
    s_wkf = nc.dram_tensor("s_wkf", [C, FFN], BF16, kind="Internal").ap()
    s_wvf = nc.dram_tensor("s_wvf", [FFN, C], BF16, kind="Internal").ap()
    s_w6 = nc.dram_tensor("s_w6", [6 * C, C], BF16, kind="Internal").ap()

    y_out = nc.dram_tensor("y", [T_FULL, C], F32, kind="ExternalOutput").ap()

    dbg = {}
    if DEBUG:
        for nm, shape in [
            ("d_n0", [TC, C]), ("d_h0", [TC, C]), ("d_xa", [C, TC]),
            ("d_xx", [C, TC]), ("d_xw", [C, TC]), ("d_xk", [C, TC]),
            ("d_r", [C, TC]), ("d_k", [C, TC]), ("d_v", [TC, C]),
            ("d_ew", [C, TC]), ("d_g", [TC, C]), ("d_y3", [C, TC]),
            ("d_att", [TC, C]), ("d_xc", [C, TC]), ("d_kx", [C, TC]),
            ("d_sig", [TC, C]), ("d_kk", [FFN, TC]), ("d_hf", [C, TC]),
            ("d_d2", [TC, C]), ("d_accH", [TC, C]), ("d_y1", [FH, TC]),
            ("d_accA", [TC, C]), ("d_st", [128, 16]),
            ("d_cs", [128, 1024]), ("d_F1", [128, 1024]), ("d_F2", [128, 1024]),
            ("d_Kh", [128, 1024]), ("d_Rt", [128, 1024]), ("d_ktm", [128, 1024]),
            ("d_mt", [128, 2048]), ("d_Sf", [128, 512]), ("d_gn", [128, 64]),
            ("d_y3l", [128, 1024]), ("d_ysq", [128, 1024]),
        ]:
            dbg[nm] = nc.dram_tensor(nm, shape, F32, kind="ExternalOutput").ap()

    es = contextlib.ExitStack()
    ec = es.enter_context

    U = 4096
    NUNITS = 14
    bfa = ec(nc.sbuf_tensor("bfa", [128, NUNITS * U], BF16))

    def unit(i, j=None):
        j = i + 1 if j is None else j
        return bfa[:, i * U:j * U]

    U_NRM, U_N0N, U_H0, U_HF = 0, 1, 2, 3
    U_XA, U_XX, U_XF0, U_XF1 = 4, 5, 6, 7
    U_XG, U_R, U_K, U_V, U_G, U_EW = 8, 9, 10, 11, 12, 13

    def K(i):
        return f"U{i}"

    f32a = ec(nc.sbuf_tensor("f32a", [128, 4096], F32))
    acc = ec(nc.sbuf_tensor("acc", [128, NTT * C], F32))
    S_f = ec(nc.sbuf_tensor("S_f", [128, 512], F32))
    S_b = ec(nc.sbuf_tensor("S_b", [128, 512], BF16))
    mtdd = ec(nc.sbuf_tensor("mtdd", [128, 2048], BF16))
    ysq = mtdd[:, 0:1024]
    sfb = mtdd[:, 0:512]
    t1b = ec(nc.sbuf_tensor("t1b", [64, 512], BF16))
    zb = ec(nc.sbuf_tensor("zb", [128, 1024], BF16))
    z0b = zb[:, 0:512]
    z1b = zb[0:32, 512:1024]
    arena = [ec(nc.sbuf_tensor(f"arena{i}", [128, 8192], BF16)) for i in range(2)]

    maskQ = ec(nc.sbuf_tensor("maskQ", [128, 512], BF16))
    ident = ec(nc.sbuf_tensor("ident", [128, 128], BF16))
    w0bc = ec(nc.sbuf_tensor("w0bc", [128, C], BF16))
    b0bc = ec(nc.sbuf_tensor("b0bc", [128, C], BF16))
    lnxwbc = ec(nc.sbuf_tensor("lnxwbc", [128, C], BF16))
    lnxbbc = ec(nc.sbuf_tensor("lnxbbc", [128, C], BF16))
    tmw1 = ec(nc.sbuf_tensor("tmw1", [128, NFT * TMDIM], BF16))
    w2a = ec(nc.sbuf_tensor("w2a", [128, C], BF16))
    w2b = ec(nc.sbuf_tensor("w2b", [32, C], BF16))
    tdw1 = ec(nc.sbuf_tensor("tdw1", [128, NFT * TD], BF16))
    tdw2 = ec(nc.sbuf_tensor("tdw2", [64, C], BF16))
    ones128 = ec(nc.sbuf_tensor("ones128", [128, 128], BF16))

    VC = {}
    vec_srcs = {
        "ln1w": ln1_w, "ln1b": ln1_b, "ln2w": ln2_w, "ln2b": ln2_b,
        "xmaa": maa["x"], "wmaa": maa["w"], "kmaa": maa["k"],
        "vmaa": maa["v"], "rmaa": maa["r"], "gmaa": maa["g"],
        "tdec": time_decay, "fkmaa": fk_maa, "frmaa": fr_maa,
        "ffng": ffn_g, "ffnb": ffn_b,
    }
    for nm in list(vec_srcs) + ["ucol", "fc1b"]:
        w = FH // 128 if nm == "fc1b" else NFT
        VC[nm] = ec(nc.sbuf_tensor("vc_" + nm, [128, w], F32))

    bn6 = ec(nc.sbuf_tensor("bn6", [128, 12], F32))
    mv4 = ec(nc.sbuf_tensor("mv4", [128, 2 * NTT], F32))
    rs4 = ec(nc.sbuf_tensor("rs4", [128, NTT], F32))
    nb4 = ec(nc.sbuf_tensor("nb4", [128, NTT], F32))
    rs4b = ec(nc.sbuf_tensor("rs4b", [128, NTT], F32))
    gn16 = ec(nc.sbuf_tensor("gn16", [128, 64], F32))
    AL8 = ec(nc.sbuf_tensor("AL8", [128, NFT], F32))
    xacar = ec(nc.sbuf_tensor("xacar", [128, NFT], BF16))
    epsln = ec(nc.sbuf_tensor("epsln", [128, 1], F32))
    epsgn = ec(nc.sbuf_tensor("epsgn", [128, 1], F32))
    b4k = ec(nc.sbuf_tensor("b4k", [128, 1], F32))
    xccar = ec(nc.sbuf_tensor("xccar", [128, NFT], BF16))

    QD0 = ec(nc.psum_tensor("QD0", [128, 512], F32))
    QD1 = ec(nc.psum_tensor("QD1", [128, 512], F32))
    QD2 = ec(nc.psum_tensor("QD2", [128, 512], F32))
    QD3 = ec(nc.psum_tensor("QD3", [128, 512], F32))
    PST = ec(nc.psum_tensor("PST", [128, 1024], BF16))
    YY = ec(nc.psum_tensor("YY", [128, 1024], F32))
    SS = ec(nc.psum_tensor("SS", [128, 512], F32))

    sems = {e: ec(nc.semaphore("sem" + e))
            for e in ["T", "V", "S", "G"]
            + [f"Y{i}" for i in range(8)] + [f"D{i}" for i in range(8)]}

    sc = Sched()

    def prologue():
        sc.op("V", lambda e: e.memset(ones128[:], 1.0), writes=["ones"])
        sc.op("V", lambda e: e.memset(epsln[:], LN_EPS), writes=["epsln"])
        sc.op("V", lambda e: e.memset(epsgn[:], GN_EPS * HN * HN),
              writes=["epsgn"])
        sc.op("V", lambda e: e.memset(b4k[:], 0.5 * np.log(float(HN * HN))),
              writes=["b4k"])
        sc.op("G", lambda e: e.memset(S_f[:], 0.0), writes=["S_f"])
        sc.op("G", lambda e: e.memset(S_b[:], 0.0), writes=["S_b"])
        sc.op("G", lambda e: e.memset(xacar[:], 0.0), writes=["xacar"])
        sc.op("G", lambda e: e.memset(xccar[:], 0.0), writes=["xccar"])
        sc.op("Y", lambda e: e.dma_start(maskQ[:], cmaskQ), writes=["maskQ"], dma=True)
        sc.op("Y", lambda e: e.dma_start(ident[:], cident), writes=["ident"], dma=True)
        sc.op("G", lambda e: e.dma_start(w0bc[:], cw0bc), writes=["w0bc"], dma=True)
        sc.op("G", lambda e: e.dma_start(b0bc[:], cb0bc), writes=["b0bc"], dma=True)
        sc.op("G", lambda e: e.dma_start(lnxwbc[:], clnxw), writes=["lnxw"], dma=True)
        sc.op("G", lambda e: e.dma_start(lnxbbc[:], clnxb), writes=["lnxb"], dma=True)
        for nm, src in vec_srcs.items():
            ap = src.rearrange("(t p) -> p t", p=128)
            sc.op("Y", lambda e, ap=ap, nm=nm: e.dma_start(
                VC[nm][:], ap, allow_slow_non_contiguous=True),
                writes=["vc_" + nm], dma=True)
        apu = time_faaaa.rearrange("h n -> (h n)").rearrange("(t p) -> p t", p=128)
        sc.op("Y", lambda e: e.dma_start(VC["ucol"][:], apu,
                                         allow_slow_non_contiguous=True),
              writes=["vc_ucol"], dma=True)
        apb = fc1_bias.rearrange("(t p) -> p t", p=128)
        sc.op("Y", lambda e: e.dma_start(VC["fc1b"][:], apb,
                                         allow_slow_non_contiguous=True),
              writes=["vc_fc1b"], dma=True)
        sc.op("G", lambda e: e.dma_start(
            tmw1[:].rearrange("p (kt m) -> p kt m", kt=NFT),
            tm_w1.rearrange("(kt p) m -> p kt m", p=128)),
            writes=["tmw1"], dma=True)
        for f in range(4):
            sc.op("G", lambda e, f=f: e.dma_start(w2a[ts(f, 32), :], tm_w2[f]),
                  writes=["w2a"], dma=True)
        sc.op("G", lambda e: e.dma_start(w2b[:], tm_w2[4]), writes=["w2b"], dma=True)
        sc.op("G", lambda e: e.dma_start(
            tdw1[:].rearrange("p (kt m) -> p kt m", kt=NFT),
            td_w1.rearrange("(kt p) m -> p kt m", p=128)),
            writes=["tdw1"], dma=True)
        sc.op("G", lambda e: e.dma_start(tdw2[:], td_w2), writes=["tdw2"], dma=True)

        for wi, wsrc in enumerate((Wk, Wv, Wr, Wg, Wo, Wr_f)):
            sc.op("G", lambda e, wsrc=wsrc: e.dma_start(
                arena[0][:, :8192].rearrange("p (kt n) -> p kt n", kt=NFT),
                wsrc.rearrange("(kt p) n -> p kt n", p=128)),
                writes=["slot0"], dma=True)
            sc.op("Y", lambda e, wi=wi: e.dma_start(
                s_w6[wi * C:(wi + 1) * C, :]
                .rearrange("(kt p) n -> p kt n", p=128),
                arena[0][:, :8192].rearrange("p (kt n) -> p kt n", kt=NFT)),
                reads=["slot0"], writes=["s_w6"], dma=True)
        stg = unit(U_EW)[:, :FFN]
        for kt in range(NFT):
            sc.op("G", lambda e, kt=kt: e.dma_start(stg, Wk_f[ts(kt), :]),
                  writes=[K(U_EW)], dma=True)
            sc.op("Y", lambda e, kt=kt: e.dma_start(s_wkf[ts(kt), :], stg),
                  reads=[K(U_EW)], writes=["s_wkf"], dma=True)
        stg2 = unit(U_EW)[:, :C]
        for kt in range(FFN // 128):
            sc.op("G", lambda e, kt=kt: e.dma_start(stg2, Wv_f[ts(kt), :]),
                  writes=[K(U_EW)], dma=True)
            sc.op("Y", lambda e, kt=kt: e.dma_start(s_wvf[ts(kt), :], stg2),
                  reads=[K(U_EW)], writes=["s_wvf"], dma=True)
        qv = unit(0, 2).bitcast(I32)
        sv = unit(2, 4).bitcast(F32)
        qf = unit(4, 6).bitcast(F32)
        ov = unit(6)[:, :FH]
        kq = [K(0), K(1)]
        ks = [K(2), K(3)]
        kf = [K(4), K(5)]
        ko = [K(6)]
        for kt in range(NFT):
            sc.op("Y", lambda e, kt=kt: e.dma_start(qv[:, :FH], fc1_q[ts(kt), :]),
                  writes=kq, dma=True)
            sc.op("Y", lambda e, kt=kt: e.dma_start(sv[:, :FH], cfc1s[ts(kt), :]),
                  writes=ks, dma=True)
            sc.op("V", lambda e: e.tensor_scalar_add(qf[:, :FH], qv[:, :FH], -128.0),
                  reads=kq, writes=kf)
            sc.op("V", lambda e: e.tensor_tensor(ov, qf[:, :FH], sv[:, :FH], OP.mult),
                  reads=kf + ks, writes=ko)
            sc.op("Y", lambda e, kt=kt: e.dma_start(s_fc1[ts(kt), :], ov),
                  reads=ko, writes=["s_fc1"], dma=True)
        for kt in range(FH // 128):
            sc.op("Y", lambda e, kt=kt: e.dma_start(qv[:, :C], fc2_q[ts(kt), :]),
                  writes=kq, dma=True)
            sc.op("Y", lambda e, kt=kt: e.dma_start(sv[:, :C], cfc2s[ts(kt), :]),
                  writes=ks, dma=True)
            sc.op("V", lambda e: e.tensor_scalar_add(qf[:, :C], qv[:, :C], -128.0),
                  reads=kq, writes=kf)
            sc.op("V", lambda e: e.tensor_tensor(
                ov[:, :C], qf[:, :C], sv[:, :C], OP.mult),
                reads=kf + ks, writes=ko)
            sc.op("Y", lambda e, kt=kt: e.dma_start(s_fc2[ts(kt), :], ov[:, :C]),
                  reads=ko, writes=["s_fc2"], dma=True)

    piece_defs = []
    for wi, nm in enumerate(("Wk", "Wv", "Wr", "Wg", "Wo", "Wrf")):
        piece_defs.append((nm, "G",
                           s_w6[wi * C:(wi + 1) * C, :]
                           .rearrange("(kt p) n -> p kt n", p=128), 8192))
    for q in range(4):
        piece_defs.append((f"wkf{q}", "Y",
                           s_wkf[:, q * 896:(q + 1) * 896]
                           .rearrange("(kt p) n -> p kt n", p=128), 7168))
    for hh in range(2):
        for q in range(4):
            piece_defs.append((f"wvf{q}h{hh}", "Y",
                               s_wvf[q * 896:(q + 1) * 896, :]
                               .rearrange("(kt p) n -> p kt n", p=128), 7168))
    for q in range(4):
        piece_defs.append((f"fc1{q}", "Y",
                           s_fc1[:, q * C:(q + 1) * C]
                           .rearrange("(kt p) n -> p kt n", p=128), 8192))
    for hh in range(2):
        for q in range(4):
            piece_defs.append((f"fc2{q}h{hh}", "Y",
                               s_fc2[q * C:(q + 1) * C, :]
                               .rearrange("(kt p) n -> p kt n", p=128), 8192))
    NPIECE = len(piece_defs)
    use_order = (["Wk", "Wv", "Wr", "Wg"]
                 + [f"fc1{q}" for q in range(4)]
                 + ["Wo"]
                 + [f"fc2{q}h{hh}" for hh in range(2) for q in range(4)]
                 + ["Wrf"]
                 + [f"wkf{q}" for q in range(4)]
                 + [f"wvf{q}h{hh}" for hh in range(2) for q in range(4)])
    pd = {nm: (qq, ap, cols) for nm, qq, ap, cols in piece_defs}
    piece_slot = {}
    seq = [(ch, nm) for ch in range(NCH) for nm in use_order]
    seq_pos = {key: i for i, key in enumerate(seq)}

    def _load(key):
        if key in piece_slot:
            return
        ch, nm = key
        qq, ap, cols = pd[nm]
        qq = "G" if seq_pos[key] % 2 == 0 else "Y"
        slot = seq_pos[key] % 2
        piece_slot[key] = slot
        nkt = ap.shape[1]
        dst = arena[slot][:, :cols].rearrange("p (kt n) -> p kt n", kt=nkt)
        rds = []
        for pref, sk in (("wkf", "s_wkf"), ("wvf", "s_wvf"),
                         ("fc1", "s_fc1"), ("fc2", "s_fc2"),
                         ("W", "s_w6")):
            if nm.startswith(pref):
                rds = [sk]
        sc.op(qq, lambda e, ap=ap, dst=dst: e.dma_start(dst, ap),
              reads=rds, writes=[f"slot{slot}"], dma=True)

    def piece(ch, nm):
        key = (ch, nm)
        _load(key)
        nxt = seq_pos[key] + 1
        if nxt < len(seq):
            _load(seq[nxt])
        slot = piece_slot[key]
        return arena[slot], f"slot{slot}"

    def ln_stats(src, srcbuf):
        for tt in range(NTT):
            a = src[:, tt * C: tt * C + 512]
            b = src[:, tt * C + 512: (tt + 1) * C]
            sc.op("V", lambda e, a=a: e.bn_stats(bn6[:, 0:6], a),
                  reads=[srcbuf], writes=["bn6"], lazy=True)
            sc.op("V", lambda e, b=b: e.bn_stats(bn6[:, 6:12], b),
                  reads=[srcbuf], writes=["bn6"], lazy=True)
            sc.op("V", lambda e, tt=tt: e.bn_aggr(mv4[:, 2 * tt:2 * tt + 2],
                                                  bn6[:]),
                  reads=["bn6"], writes=["mv4"], lazy=True)
        var2 = mv4[:].rearrange("p (t two) -> p t two", two=2)[:, :, 1:2] \
            .rearrange("p t one -> p (t one)")
        mu2 = mv4[:].rearrange("p (t two) -> p t two", two=2)[:, :, 0:1] \
            .rearrange("p t one -> p (t one)")
        sc.op("V", lambda e, var2=var2: e.tensor_copy(rs4[:], var2),
              reads=["mv4"], writes=["rs4"], lazy=True)
        sc.op("S", lambda e: e.activation(rs4[:], rs4[:], AF.Ln,
                                          bias=epsln[:, 0:1]),
              reads=["rs4", "epsln"], writes=["rs4"])
        sc.op("S", lambda e: e.activation(rs4[:], rs4[:], AF.Exp, scale=-0.5),
              reads=["rs4"], writes=["rs4"])
        sc.op("V", lambda e, mu2=mu2: e.scalar_tensor_tensor(
            nb4[:], mu2, -1.0, rs4[:], OP.mult, OP.mult),
            reads=["mv4", "rs4"], writes=["nb4"])

    def transpose_affine(src_tm, srcbuf, dst_fm, dstbuf, scale_col, bias_col,
                         scalebuf=None, biasbuf=None):
        for tt in range(NTT):
            for ft in range(NFT):
                sc.op("T", lambda e, tt=tt, ft=ft, src_tm=src_tm: e.transpose(
                    PST[:, ts(ft)],
                    src_tm[:, tt * C + ft * 128: tt * C + (ft + 1) * 128],
                    ident[:]),
                    reads=[srcbuf, "ident"], writes=["PST"])
            for ft in range(NFT):
                d = dst_fm[:, ft * TC + tt * 128: ft * TC + (tt + 1) * 128]
                s0 = scale_col[:, ft:ft + 1] if scale_col is not None else 1.0
                b0 = bias_col[:, ft:ft + 1] if bias_col is not None else 0.0
                rds = ["PST"] + ([scalebuf] if scalebuf else []) + (
                    [biasbuf] if biasbuf else [])
                sc.op("S", lambda e, d=d, ft=ft, s0=s0, b0=b0: e.activation(
                    d, PST[:, ts(ft)], AF.Identity, scale=s0, bias=b0),
                    reads=rds, writes=[dstbuf])

    def dump_fm(name, src, srcbuf, ch=None):
        if not DEBUG or (DUMP_CH is not None and ch is not None
                         and ch != DUMP_CH):
            return
        if DUMP_KEYS is not None and name not in DUMP_KEYS:
            return
        sc.op("G", lambda e, src=src, name=name: e.dma_start(
            dbg[name].rearrange("(ft p) t -> p ft t", p=128),
            src.rearrange("p (ft t) -> p ft t", t=TC)),
            reads=[srcbuf], dma=True)

    def dump_tm(name, src, srcbuf, ch=None):
        if not DEBUG or (DUMP_CH is not None and ch is not None
                         and ch != DUMP_CH):
            return
        if DUMP_KEYS is not None and name not in DUMP_KEYS:
            return
        sc.op("G", lambda e, src=src, name=name: e.dma_start(
            dbg[name].rearrange("(tt p) c -> p tt c", p=128),
            src.rearrange("p (tt c) -> p tt c", c=C)),
            reads=[srcbuf], dma=True)

    def chunk(ch):
        t0g = ch * TC
        _load(seq[ch * len(use_order)])

        x = f32a[:, 0:4096]
        y1units = [U_NRM, U_XX, U_XF1, U_XG]

        def y1sl(gkt, tt=None):
            un = y1units[gkt // 8]
            base = un * U + (gkt % 8) * TC
            if tt is None:
                return bfa[:, base:base + TC], K(un)
            return bfa[:, base + tt * 128: base + (tt + 1) * 128], K(un)
        sc.op("G", lambda e: e.dma_start(
            x.rearrange("p (tt c) -> p tt c", tt=NTT),
            x_in[t0g:t0g + TC, :].rearrange("(tt p) c -> p tt c", p=128)),
            writes=["f32a", "f32cs", "f32gn"], dma=True)
        ln_stats(x, "f32a")
        sc.op("V", lambda e: e.tensor_scalar_add(rs4b[:], rs4[:], 1.0),
              reads=["rs4"], writes=["rs4b"], lazy=True)
        n0 = unit(U_NRM)
        for tt in range(NTT):
            sc.op("S", lambda e, tt=tt: e.activation(
                acc[:, ts(tt, C)], x[:, ts(tt, C)], AF.Identity,
                scale=rs4b[:, tt:tt + 1], bias=nb4[:, tt:tt + 1]),
                reads=["f32a", "rs4b", "nb4"], writes=["acc"])
            sc.op("S", lambda e, tt=tt: e.activation(
                n0[:, ts(tt, C)], x[:, ts(tt, C)], AF.Identity,
                scale=rs4[:, tt:tt + 1], bias=nb4[:, tt:tt + 1]),
                reads=["f32a", "rs4", "nb4"], writes=[K(U_NRM)])
        dump_tm("d_n0", n0, K(U_NRM), ch)
        dump_tm("d_accA", acc, "acc", ch)
        if (DEBUG and (DUMP_CH is None or ch == DUMP_CH)
                and (DUMP_KEYS is None or "d_st" in DUMP_KEYS)):
            sc.op("G", lambda e: e.dma_start(dbg["d_st"][:, 0:8], mv4[:]),
                  reads=["mv4"], dma=True)
            sc.op("G", lambda e: e.dma_start(dbg["d_st"][:, 8:12], rs4[:]),
                  reads=["rs4"], dma=True)
            sc.op("G", lambda e: e.dma_start(dbg["d_st"][:, 12:16], nb4[:]),
                  reads=["nb4"], dma=True)
        ln_stats(n0, K(U_NRM))
        n0n = unit(U_N0N)
        for tt in range(NTT):
            sc.op("S", lambda e, tt=tt: e.activation(
                n0n[:, ts(tt, C)], n0[:, ts(tt, C)], AF.Identity,
                scale=rs4[:, tt:tt + 1], bias=nb4[:, tt:tt + 1]),
                reads=[K(U_NRM), "rs4", "nb4"], writes=[K(U_N0N)])
        h0 = unit(U_H0)
        for tt in range(NTT):
            sc.op("V", lambda e, tt=tt: e.tensor_tensor(
                h0[:, ts(tt, C)], n0n[:, ts(tt, C)], w0bc[:], OP.mult),
                reads=[K(U_N0N), "w0bc"], writes=[K(U_H0)])
            sc.op("V", lambda e, tt=tt: e.tensor_tensor(
                h0[:, ts(tt, C)], h0[:, ts(tt, C)], b0bc[:], OP.add),
                reads=["b0bc"], writes=[K(U_H0)])
        dump_tm("d_h0", h0, K(U_H0), ch)
        hf = unit(U_HF)
        transpose_affine(n0n, K(U_N0N), hf, K(U_HF), VC["ffng"], VC["ffnb"],
                         "vc_ffng", "vc_ffnb")
        dump_fm("d_hf", hf, K(U_HF), ch)
        ln_stats(h0, K(U_H0))
        na1 = unit(U_NRM)
        for tt in range(NTT):
            sc.op("S", lambda e, tt=tt: e.activation(
                na1[:, ts(tt, C)], h0[:, ts(tt, C)], AF.Identity,
                scale=rs4[:, tt:tt + 1], bias=nb4[:, tt:tt + 1]),
                reads=[K(U_H0), "rs4", "nb4"], writes=[K(U_NRM)])
        xa = unit(U_XA)
        transpose_affine(na1, K(U_NRM), xa, K(U_XA), VC["ln1w"], VC["ln1b"],
                         "vc_ln1w", "vc_ln1b")
        dump_fm("d_xa", xa, K(U_XA), ch)
        xx = unit(U_XX)
        xa3 = xa.rearrange("p (ft t) -> p ft t", t=TC)
        xx3 = xx.rearrange("p (ft t) -> p ft t", t=TC)
        sc.op("V", lambda e: e.tensor_tensor(
            xx3[:, :, 1:TC], xa3[:, :, 0:TC - 1], xa3[:, :, 1:TC], OP.subtract),
            reads=[K(U_XA)], writes=[K(U_XX)])
        sc.op("V", lambda e: e.tensor_tensor(
            xx3[:, :, 0:1], xacar[:].unsqueeze(2), xa3[:, :, 0:1], OP.subtract),
            reads=[K(U_XA), "xacar"], writes=[K(U_XX)])
        sc.op("V", lambda e: e.tensor_copy(
            xacar[:].unsqueeze(2), xa3[:, :, TC - 1:TC]),
            reads=[K(U_XA)], writes=["xacar"], lazy=True)
        dump_fm("d_xx", xx, K(U_XX), ch)

        if 'B' not in PHASES:
            return
        t0 = unit(U_XF0)
        t03 = t0.rearrange("p (ft t) -> p ft t", t=TC)
        for ft in range(NFT):
            sc.op("V", lambda e, ft=ft: e.scalar_tensor_tensor(
                t03[:, ft], xx3[:, ft], VC["xmaa"][:, ft:ft + 1], xa3[:, ft],
                OP.mult, OP.add),
                reads=[K(U_XX), K(U_XA), "vc_xmaa"], writes=[K(U_XF0)])
        for mt, (zb, zbuf, mrows) in enumerate(
                [(z0b, "z0", 128), (z1b, "z1", 32)]):
            for kt in range(NFT):
                sc.op("T", lambda e, mt=mt, kt=kt, mrows=mrows: e.matmul(
                    QD0[0:mrows, 0:TC],
                    tmw1[:, kt * TMDIM + mt * 128: kt * TMDIM + mt * 128 + mrows],
                    t03[:, kt], start=(kt == 0), stop=(kt == NFT - 1)),
                    reads=["tmw1", K(U_XF0)], writes=["QD0"])
            sc.op("S", lambda e, zb=zb, mrows=mrows: e.activation(
                zb[:], QD0[0:mrows, 0:TC], AF.Tanh), reads=["QD0"], writes=[zbuf])

        projs = [("w", "wmaa", None), ("k", "kmaa", "Wk"), ("v", "vmaa", "Wv"),
                 ("r", "rmaa", "Wr"), ("g", "gmaa", "Wg")]
        r_fm = unit(U_R); k_fm = unit(U_K); v_tm = unit(U_V); g_tm = unit(U_G)
        ew = unit(U_EW)
        for fi, (fn_, maanm, wnm) in enumerate(projs):
            un = U_XF0 if fi % 2 == 0 else U_XF1
            xf = unit(un)
            xfbuf = K(un)
            xf3 = xf.rearrange("p (ft t) -> p ft t", t=TC)
            zsl = z0b[ts(fi % 4, 32), :] if fi < 4 else z1b[:, :]
            zbuf = "z0" if fi < 4 else "z1"
            zbase = (fi % 4) * 32 if fi < 4 else 0
            w2buf = "w2a" if fi < 4 else "w2b"
            for ft in range(NFT):
                wt = (w2a[ts(fi % 4, 32), ts(ft)] if fi < 4 else w2b[:, ts(ft)])
                QP, qpb = (QD1, "QD1") if ft % 2 == 0 else (QD0, "QD0")
                sfv = mtdd[:, (ft % 2) * 512:(ft % 2) * 512 + 512]
                sc.op("T", lambda e, wt=wt, zsl=zsl, zbase=zbase, QP=QP:
                      e.matmul(
                    QP[:, 0:TC], wt, zsl, start=True, stop=True,
                    tile_position=(zbase, 0)),
                    reads=[w2buf, zbuf], writes=[qpb])
                sc.op("S", lambda e, ft=ft, maanm=maanm, QP=QP, sfv=sfv:
                      e.activation(
                    sfv, QP[:, 0:TC], AF.Identity,
                    bias=VC[maanm][:, ft:ft + 1]),
                    reads=[qpb, "vc_" + maanm], writes=["mtdd"])
                sc.op("V", lambda e, ft=ft, xf3=xf3, sfv=sfv: e.tensor_tensor(
                    xf3[:, ft], xx3[:, ft], sfv, OP.mult),
                    reads=[K(U_XX), "mtdd"], writes=[xfbuf])
                sc.op("V", lambda e, ft=ft, xf3=xf3: e.tensor_tensor(
                    xf3[:, ft], xf3[:, ft], xa3[:, ft], OP.add),
                    reads=[K(U_XA)], writes=[xfbuf])
            if fn_ == "w":
                dump_fm("d_xw", xf, xfbuf, ch)
                for kt in range(NFT):
                    sc.op("T", lambda e, kt=kt, xf3=xf3: e.matmul(
                        QD0[0:64, 0:TC], tdw1[:, kt * TD:(kt + 1) * TD],
                        xf3[:, kt], start=(kt == 0), stop=(kt == NFT - 1)),
                        reads=["tdw1", xfbuf], writes=["QD0"])
                sc.op("S", lambda e: e.activation(t1b[:], QD0[0:64, 0:TC],
                                                  AF.Tanh),
                      reads=["QD0"], writes=["t1b"])
                for ft in range(NFT):
                    sc.op("T", lambda e, ft=ft: e.matmul(
                        QD1[:, 0:TC], tdw2[:, ts(ft)], t1b[:],
                        start=True, stop=True),
                        reads=["tdw2", "t1b"], writes=["QD1"])
                    sc.op("S", lambda e, ft=ft: e.activation(
                        ew[:, ft * TC:(ft + 1) * TC], QD1[:, 0:TC], AF.Exp,
                        bias=VC["tdec"][:, ft:ft + 1]),
                        reads=["QD1", "vc_tdec"], writes=[K(U_EW)])
                dump_fm("d_ew", ew, K(U_EW), ch)
                continue
            if fn_ == "k":
                dump_fm("d_xk", xf, xfbuf, ch)
            wq, wqbuf = piece(ch, wnm)
            if fn_ in ("k", "r"):
                dst = k_fm if fn_ == "k" else r_fm
                dbuf = K(U_K) if fn_ == "k" else K(U_R)
                for mt in range(NFT):
                    QP, qpb = (QD2, "QD2") if mt % 2 == 0 else (QD3, "QD3")
                    for kt in range(NFT):
                        sc.op("T", lambda e, mt=mt, kt=kt, wq=wq, xf3=xf3,
                              QP=QP: e.matmul(
                                  QP[:, 0:TC],
                                  wq[:, kt * C + mt * 128: kt * C + (mt + 1) * 128],
                                  xf3[:, kt], start=(kt == 0),
                                  stop=(kt == NFT - 1)),
                              reads=[wqbuf, xfbuf], writes=[qpb])
                    sc.op("S", lambda e, mt=mt, dst=dst, QP=QP: e.activation(
                        dst[:, mt * TC:(mt + 1) * TC], QP[:, 0:TC],
                        AF.Identity), reads=[qpb], writes=[dbuf])
                dump_fm("d_" + fn_, dst, dbuf, ch)
            else:
                dst = v_tm if fn_ == "v" else g_tm
                dbuf = K(U_V) if fn_ == "v" else K(U_G)
                act_f = AF.Identity if fn_ == "v" else AF.Silu
                for tt in range(NTT):
                    for nh in range(2):
                        QP, qpb = (QD2, "QD2") if (2 * tt + nh) % 2 == 0 \
                            else (QD3, "QD3")
                        for kt in range(NFT):
                            sc.op("T", lambda e, tt=tt, nh=nh, kt=kt, wq=wq,
                                  xf=xf, QP=QP: e.matmul(
                                      QP[:, 0:512],
                                      xf[:, kt * TC + tt * 128:
                                         kt * TC + (tt + 1) * 128],
                                      wq[:, kt * C + nh * 512:
                                         kt * C + (nh + 1) * 512],
                                      start=(kt == 0), stop=(kt == NFT - 1)),
                                  reads=[wqbuf, xfbuf], writes=[qpb])
                        sc.op("S", lambda e, tt=tt, nh=nh, dst=dst,
                              act_f=act_f, QP=QP: e.activation(
                                  dst[:, tt * C + nh * 512:
                                      tt * C + (nh + 1) * 512],
                                  QP[:, 0:512], act_f),
                              reads=[qpb], writes=[dbuf])
                dump_tm("d_" + fn_, dst, dbuf, ch)

        if 'C' not in PHASES:
            return
        csl = f32a[:, 0:1024]
        f1t = f32a[:, 1024:2048]
        rbb = f32a[:, 2048:3072]
        mrb = f32a[:, 3072:4096]
        pk1 = unit(U_N0N)
        F2 = pk1[:, 0:1024]; F1 = pk1[:, 1024:2048]
        Kh = pk1[:, 2048:3072]; uk = pk1[:, 3072:4096]
        pk2 = unit(U_XF0)
        Rt = pk2[:, 0:1024]; ktm = pk2[:, 2048:3072]; y3l = pk2[:, 3072:4096]
        y3 = unit(U_XA)
        ew3 = ew.rearrange("p (ft t) -> p ft t", t=TC)
        k3 = k_fm.rearrange("p (ft t) -> p ft t", t=TC)
        r3 = r_fm.rearrange("p (ft t) -> p ft t", t=TC)
        cs3 = csl.rearrange("p (ft t) -> p ft t", t=128)
        f1t3 = f1t.rearrange("p (ft t) -> p ft t", t=128)
        F13 = F1.rearrange("p (ft t) -> p ft t", t=128)
        F23 = F2.rearrange("p (ft t) -> p ft t", t=128)
        kh3 = Kh.rearrange("p (ft t) -> p ft t", t=128)
        uk3 = uk.rearrange("p (ft t) -> p ft t", t=128)
        Rt3 = Rt.rearrange("p (ft t) -> p ft t", t=128)
        QDS = [QD0, QD1, QD2, QD3]
        QBUF = ["QD0", "QD1", "QD2", "QD3"]

        def qtd_round(r, l):
            for par, b0 in ((0, 0), (64, 2)):
                for i in range(4):
                    hp = 4 * r + i
                    bk = b0 + i // 2
                    off = (i % 2) * 256
                    sc.op("T", lambda e, hp=hp, par=par, bk=bk, off=off:
                          e.matmul(
                              QDS[bk][:, off:off + 128],
                              F2[par:par + 64, ts(hp)],
                              Rt[par:par + 64, ts(hp)],
                              start=True, stop=True,
                              tile_position=(par, 0)),
                          reads=[K(U_XF0), K(U_N0N)], writes=[QBUF[bk]])
                    sc.op("T", lambda e, hp=hp, par=par, bk=bk, off=off,
                          l=l: e.matmul(
                              QDS[bk][:, off + 128:off + 256],
                              uk[par:par + 64, ts(hp)],
                              r_fm[par:par + 64, hp * TC + l * 128:
                                   hp * TC + (l + 1) * 128],
                              start=True, stop=True,
                              tile_position=(par, 0)),
                          reads=[K(U_N0N), K(U_R)], writes=[QBUF[bk]])

        def masks_round(r, l):
            for bk in range(4):
                sc.op("V", lambda e, bk=bk: e.tensor_tensor(
                    mtdd[:, bk * 512:(bk + 1) * 512], QDS[bk][:, 0:512],
                    maskQ[:], OP.mult),
                    reads=[QBUF[bk], "maskQ"], writes=["mtdd"])

        def y_round(r, l, YP, ypbuf):
            for i in range(4):
                hp = 4 * r + i
                for hsub in range(2):
                    h = hp * 2 + hsub
                    par = hsub * 64
                    base = ((i // 2) + 2 * hsub) * 512 + (i % 2) * 256
                    vsl = v_tm[:, l * C + h * HN: l * C + (h + 1) * HN]
                    ysl = YP[:, h * HN:(h + 1) * HN]
                    sc.op("T", lambda e, base=base, vsl=vsl, ysl=ysl:
                          e.matmul(
                              ysl, mtdd[:, base:base + 128], vsl,
                              start=True, stop=False),
                          reads=["mtdd", K(U_V)], writes=[ypbuf])
                    sc.op("T", lambda e, base=base, vsl=vsl, ysl=ysl:
                          e.matmul(
                              ysl, mtdd[:, base + 128:base + 256], vsl,
                              start=False, stop=False),
                          reads=["mtdd", K(U_V)], writes=[ypbuf])
                    sc.op("T", lambda e, hp=hp, par=par, ysl=ysl: e.matmul(
                        ysl, Rt[par:par + 64, ts(hp)],
                        S_b[par:par + 64, hp * HN:(hp + 1) * HN],
                        start=False, stop=True, tile_position=(par, 0)),
                        reads=[K(U_XF0), "S_b"], writes=[ypbuf])
                    sc.op("T", lambda e, hp=hp, par=par, vsl=vsl:
                          e.matmul(
                        SS[par:par + 64, hp * HN:(hp + 1) * HN],
                        ktm[:, hp * 128 + par: hp * 128 + par + 64], vsl,
                        start=True, stop=True, tile_position=(0, par)),
                        reads=[K(U_XF0), K(U_V)], writes=["SS"])


        def wkv_front(l):
            for ft in range(NFT):
                sc.op("V", lambda e, ft=ft, l=l: e.tensor_tensor_scan(
                    cs3[:, ft], ew3[:, ft, l * 128:(l + 1) * 128],
                    ones128[:], 0.0, OP.add, OP.mult),
                    reads=[K(U_EW), "ones", "f32a"], writes=["f32cs"],
                    lazy=True)
            sc.op("V", lambda e: e.tensor_copy(
                AL8[:].unsqueeze(2), cs3[:, :, 127:128]),
                reads=["f32cs"], writes=["AL8"], lazy=True)
            sc.op("S", lambda e: e.activation(AL8[:], AL8[:], AF.Exp,
                                              scale=-1.0),
                reads=["AL8"], writes=["AL8"])
            sc.op("V", lambda e, l=l: e.tensor_tensor(
                f1t3[:], ew3[:, :, l * 128:(l + 1) * 128], cs3[:],
                OP.subtract), reads=[K(U_EW), "f32cs"], writes=["f32cs"])
            sc.op("S", lambda e: e.activation(
                pk1[:, 0:2048], f32a[:, 0:2048], AF.Exp),
                reads=["f32cs"], writes=[K(U_N0N)])
            sc.op("V", lambda e: e.tensor_tensor(
                kh3[:], F23[:],
                AL8[:].unsqueeze(2).broadcast_to([128, NFT, 128]), OP.mult),
                reads=[K(U_N0N), "AL8"], writes=[K(U_N0N)])
            sc.op("V", lambda e, l=l: e.tensor_tensor(
                kh3[:], kh3[:], k3[:, :, l * 128:(l + 1) * 128], OP.mult),
                reads=[K(U_K)], writes=[K(U_N0N)])
            sc.op("V", lambda e, l=l: e.tensor_tensor(
                F23[:], F23[:], k3[:, :, l * 128:(l + 1) * 128], OP.mult),
                reads=[K(U_K)], writes=[K(U_N0N)])
            sc.op("V", lambda e, l=l: e.tensor_tensor(
                uk3[:], k3[:, :, l * 128:(l + 1) * 128],
                VC["ucol"][:].unsqueeze(2).broadcast_to([128, NFT, 128]),
                OP.mult), reads=[K(U_K), "vc_ucol"], writes=[K(U_N0N)])
            for hp in range(NFT):
                sc.op("T", lambda e, hp=hp: e.transpose(
                    PST[:, ts(hp)], Kh[:, ts(hp)], ident[:]),
                    reads=[K(U_N0N), "ident"], writes=["PST"])
            sc.op("S", lambda e: e.activation(ktm[:], PST[:], AF.Identity),
                  reads=["PST"], writes=[K(U_XF0)])

        def wkv_rtqtd(l):
            sc.op("V", lambda e, l=l: e.tensor_tensor(
                Rt3[:], F13[:], r3[:, :, l * 128:(l + 1) * 128], OP.mult),
                reads=[K(U_R), K(U_N0N)], writes=[K(U_XF0)])
            qtd_round(0, l)
            masks_round(0, l)
            qtd_round(1, l)

        def wkv_y(l):
            YP, ypbuf = YY, "YY"
            y_round(0, l, YP, ypbuf)
            masks_round(1, l)
            y_round(1, l, YP, ypbuf)

        def wkv_gn(l):
            YP, ypbuf = YY, "YY"
            for hp in range(NFT):
                sc.op("V", lambda e, hp=hp: e.scalar_tensor_tensor(
                    S_f[:, hp * HN:(hp + 1) * HN], S_f[:, hp * HN:(hp + 1) * HN],
                    AL8[:, hp:hp + 1], SS[:, hp * HN:(hp + 1) * HN],
                    OP.mult, OP.add), reads=["AL8", "SS"], writes=["S_f"])
            sc.op("S", lambda e: e.activation(S_b[:], S_f[:], AF.Identity),
                  reads=["S_f"], writes=["S_b"])
            sc.op("V", lambda e, YP=YP: e.tensor_reduce(
                gn16[:, 0:16], YP[:].rearrange("p (h n) -> p h n", h=H),
                axis=AX.X, op=OP.add), reads=[ypbuf], writes=["gn16"],
                lazy=True)
            sc.op("S", lambda e, YP=YP: e.activation(ysq[:], YP[:], AF.Square),
                  reads=[ypbuf], writes=["mtdd"])
            sc.op("V", lambda e: e.tensor_reduce(
                gn16[:, 16:32], ysq[:].rearrange("p (h n) -> p h n", h=H),
                axis=AX.X, op=OP.add), reads=["mtdd"], writes=["gn16"],
                lazy=True)
            sc.op("V", lambda e: e.tensor_tensor(
                gn16[:, 32:48], gn16[:, 0:16], gn16[:, 0:16], OP.mult),
                reads=["gn16"], writes=["gn16"])
            sc.op("V", lambda e: e.scalar_tensor_tensor(
                gn16[:, 16:32], gn16[:, 16:32], float(HN), gn16[:, 32:48],
                OP.mult, OP.subtract),
                reads=["gn16"], writes=["gn16"])
            sc.op("S", lambda e: e.activation(
                gn16[:, 32:48], gn16[:, 16:32], AF.Ln, bias=epsgn[:, 0:1]),
                reads=["gn16", "epsgn"], writes=["gn16"])
            sc.op("S", lambda e: e.activation(
                gn16[:, 32:48], gn16[:, 32:48], AF.Exp, scale=-0.5,
                bias=b4k[:, 0:1]),
                reads=["gn16", "b4k"], writes=["gn16"])
            sc.op("V", lambda e: e.scalar_tensor_tensor(
                gn16[:, 48:64], gn16[:, 0:16], 1.0 / HN, gn16[:, 32:48],
                OP.mult, OP.mult),
                reads=["gn16"], writes=["gn16"])
            sc.op("V", lambda e: e.tensor_copy(
                rbb.rearrange("p (h n) -> p h n", h=H),
                gn16[:, 32:48].unsqueeze(2).broadcast_to([128, H, HN])),
                reads=["gn16", "f32a"], writes=["f32gn"], lazy=True)
            sc.op("V", lambda e: e.tensor_copy(
                mrb.rearrange("p (h n) -> p h n", h=H),
                gn16[:, 48:64].unsqueeze(2).broadcast_to([128, H, HN])),
                reads=["gn16", "f32a"], writes=["f32gn"], lazy=True)
            sc.op("V", lambda e, YP=YP: e.tensor_tensor(
                y3l[:], YP[:], rbb, OP.mult),
                reads=[ypbuf, "f32gn"], writes=[K(U_XF0)])
            sc.op("V", lambda e: e.tensor_tensor(y3l[:], y3l[:], mrb,
                                                 OP.subtract),
                  reads=["f32gn"], writes=[K(U_XF0)])
            sc.op("V", lambda e: e.tensor_tensor(y3l[:], y3l[:], lnxwbc[:],
                                                 OP.mult),
                  reads=["lnxw"], writes=[K(U_XF0)])
            sc.op("V", lambda e: e.tensor_tensor(y3l[:], y3l[:], lnxbbc[:],
                                                 OP.add),
                  reads=["lnxb"], writes=[K(U_XF0)])
            sc.op("V", lambda e, l=l: e.tensor_tensor(
                y3l[:], y3l[:], g_tm[:, l * C:(l + 1) * C], OP.mult),
                reads=[K(U_G)], writes=[K(U_XF0)])
            if (DEBUG and l == 0 and (DUMP_CH is None or ch == DUMP_CH)
                    and (DUMP_KEYS is None or "d_cs" in DUMP_KEYS)):
                for nm2, src2, buf2 in [
                        ("d_cs", csl, "f32a"), ("d_F1", F1, K(U_N0N)),
                        ("d_F2", F2, K(U_N0N)), ("d_Kh", Kh, K(U_N0N)),
                        ("d_Rt", Rt, K(U_XF0)), ("d_ktm", ktm, K(U_XF0)),
                        ("d_mt", mtdd[:, 0:2048], "mtdd"),
                        ("d_Sf", S_f[:], "S_f"), ("d_gn", gn16[:], "gn16"),
                        ("d_y3l", y3l, K(U_XF0)), ("d_ysq", ysq, "mtdd")]:
                    sc.op("G", lambda e, nm2=nm2, src2=src2: e.dma_start(
                        dbg[nm2], src2), reads=[buf2], dma=True)
            q = l
            wq1, wq1buf = piece(ch, f"fc1{q}")
            hf3i = hf.rearrange("p (ft t) -> p ft t", t=TC)
            for mt in range(NFT):
                gmt = q * NFT + mt
                QP, qpb = (QD2, "QD2") if gmt % 2 == 0 else (QD3, "QD3")
                ysl1, ykey = y1sl(gmt)
                for kt in range(NFT):
                    sc.op("T", lambda e, mt=mt, kt=kt, wq1=wq1, QP=QP,
                          hf3i=hf3i: e.matmul(
                        QP[:, 0:TC],
                        wq1[:, kt * C + mt * 128: kt * C + (mt + 1) * 128],
                        hf3i[:, kt], start=(kt == 0), stop=(kt == NFT - 1)),
                        reads=[wq1buf, K(U_HF)], writes=[qpb])
                sc.op("S", lambda e, gmt=gmt, QP=QP, ysl1=ysl1: e.activation(
                    ysl1, QP[:, 0:TC], AF.Identity,
                    bias=VC["fc1b"][:, gmt:gmt + 1]),
                    reads=[qpb, "vc_fc1b"], writes=[ykey])
            for ft in range(NFT):
                sc.op("T", lambda e, ft=ft: e.transpose(
                    PST[:, ts(ft)], y3l[:, ts(ft)], ident[:]),
                    reads=[K(U_XF0), "ident"], writes=["PST"])
            dst = y3.rearrange("p (ft t) -> p ft t", t=TC)[:, :,
                                                           l * 128:(l + 1) * 128]
            sc.op("S", lambda e, dst=dst: e.activation(
                dst, PST[:].rearrange("p (ft t) -> p ft t", t=128),
                AF.Identity), reads=["PST"], writes=[K(U_XA)])

        wkv_front(0)
        wkv_rtqtd(0)
        for l in range(NL):
            wkv_y(l)
            if l + 1 < NL:
                wkv_front(l + 1)
            wkv_gn(l)
            if l + 1 < NL:
                wkv_rtqtd(l + 1)
        dump_fm("d_y3", y3, K(U_XA), ch)

        if 'E' not in PHASES:
            return
        att = unit(U_EW)
        wq, wqbuf = piece(ch, "Wo")
        for tt in range(NTT):
            for nh in range(2):
                QP, qpb = (QD2, "QD2") if (2 * tt + nh) % 2 == 0 \
                    else (QD3, "QD3")
                for kt in range(NFT):
                    sc.op("T", lambda e, tt=tt, nh=nh, kt=kt, wq=wq, QP=QP:
                          e.matmul(
                        QP[:, 0:512],
                        y3[:, kt * TC + tt * 128: kt * TC + (tt + 1) * 128],
                        wq[:, kt * C + nh * 512: kt * C + (nh + 1) * 512],
                        start=(kt == 0), stop=(kt == NFT - 1)),
                        reads=[wqbuf, K(U_XA)], writes=[qpb])
                asl = att[:, tt * C + nh * 512: tt * C + (nh + 1) * 512]
                hsl = h0[:, tt * C + nh * 512: tt * C + (nh + 1) * 512]
                accsl = acc[:, tt * C + nh * 512: tt * C + (nh + 1) * 512]
                sc.op("V", lambda e, asl=asl, hsl=hsl, QP=QP: e.tensor_tensor(
                    asl, QP[:, 0:512], hsl, OP.add),
                    reads=[qpb, K(U_H0)], writes=[K(U_EW)])
                sc.op("V", lambda e, accsl=accsl, asl=asl: e.tensor_tensor(
                    accsl, accsl, asl, OP.add),
                    reads=[K(U_EW)], writes=["acc"])
        dump_tm("d_att", att, K(U_EW), ch)

        for gkt in range(32):
            ysl1, ykey = y1sl(gkt)
            sc.op("S", lambda e, ysl1=ysl1: e.activation(ysl1, ysl1, AF.Silu),
                  reads=[ykey], writes=[ykey])
        for half in range(2):
            slices2 = [(tt, nh) for tt in (2 * half, 2 * half + 1)
                       for nh in range(2)]
            for q in range(4):
                wq2, wq2buf = piece(ch, f"fc2{q}h{half}")
                for si, (tt, nh) in enumerate(slices2):
                    ZP, zpbuf, zoff = [(YY, "YY", 0), (YY, "YY", 512),
                                       (QD0, "QD0", 0), (QD1, "QD1", 0)][si]
                    for kt in range(NFT):
                        gkt = q * NFT + kt
                        ysl1, ykey = y1sl(gkt, tt)
                        sc.op("T", lambda e, kt=kt, wq2=wq2, ZP=ZP, zoff=zoff,
                              ysl1=ysl1, nh=nh, q=q: e.matmul(
                                  ZP[:, zoff:zoff + 512], ysl1,
                                  wq2[:, kt * C + nh * 512:
                                      kt * C + (nh + 1) * 512],
                                  start=(q == 0 and kt == 0),
                                  stop=(q == 3 and kt == NFT - 1)),
                              reads=[wq2buf, ykey], writes=[zpbuf])
            for si, (tt, nh) in enumerate(slices2):
                ZP, zpbuf, zoff = [(YY, "YY", 0), (YY, "YY", 512),
                                   (QD0, "QD0", 0), (QD1, "QD1", 0)][si]
                accsl = acc[:, tt * C + nh * 512: tt * C + (nh + 1) * 512]
                sc.op("V", lambda e, accsl=accsl, ZP=ZP, zoff=zoff:
                      e.scalar_tensor_tensor(
                          accsl, ZP[:, zoff:zoff + 512], 0.5, accsl,
                          OP.mult, OP.add),
                      reads=[zpbuf], writes=["acc"])
        fc2bb = mtdd[:, 1024:2048]
        sc.op("G", lambda e: e.dma_start(fc2bb, cfc2bb),
              writes=["mtdd"], dma=True)
        for tt in range(NTT):
            sc.op("V", lambda e, tt=tt, fc2bb=fc2bb: e.scalar_tensor_tensor(
                acc[:, ts(tt, C)], fc2bb, 0.5, acc[:, ts(tt, C)],
                OP.mult, OP.add), reads=["mtdd"], writes=["acc"])

        if 'F' not in PHASES:
            return
        ln_stats(att, K(U_EW))
        nc2 = unit(U_NRM)
        for tt in range(NTT):
            sc.op("S", lambda e, tt=tt: e.activation(
                nc2[:, ts(tt, C)], att[:, ts(tt, C)], AF.Identity,
                scale=rs4[:, tt:tt + 1], bias=nb4[:, tt:tt + 1]),
                reads=[K(U_EW), "rs4", "nb4"], writes=[K(U_NRM)])
        xc = unit(U_XA)
        transpose_affine(nc2, K(U_NRM), xc, K(U_XA), VC["ln2w"], VC["ln2b"],
                         "vc_ln2w", "vc_ln2b")
        dump_fm("d_xc", xc, K(U_XA), ch)
        cxx = unit(U_XX)
        xc3 = xc.rearrange("p (ft t) -> p ft t", t=TC)
        cx3 = cxx.rearrange("p (ft t) -> p ft t", t=TC)
        sc.op("V", lambda e: e.tensor_tensor(
            cx3[:, :, 1:TC], xc3[:, :, 0:TC - 1], xc3[:, :, 1:TC], OP.subtract),
            reads=[K(U_XA)], writes=[K(U_XX)])
        sc.op("V", lambda e: e.tensor_tensor(
            cx3[:, :, 0:1], xccar[:].unsqueeze(2), xc3[:, :, 0:1], OP.subtract),
            reads=[K(U_XA), "xccar"], writes=[K(U_XX)])
        sc.op("V", lambda e: e.tensor_copy(
            xccar[:].unsqueeze(2), xc3[:, :, TC - 1:TC]),
            reads=[K(U_XA)], writes=["xccar"], lazy=True)
        kx = unit(U_XF0)
        rx = unit(U_XF1)
        kx3 = kx.rearrange("p (ft t) -> p ft t", t=TC)
        rx3 = rx.rearrange("p (ft t) -> p ft t", t=TC)
        for ft in range(NFT):
            sc.op("V", lambda e, ft=ft: e.scalar_tensor_tensor(
                kx3[:, ft], cx3[:, ft], VC["fkmaa"][:, ft:ft + 1], xc3[:, ft],
                OP.mult, OP.add),
                reads=[K(U_XX), K(U_XA), "vc_fkmaa"], writes=[K(U_XF0)])
            sc.op("V", lambda e, ft=ft: e.scalar_tensor_tensor(
                rx3[:, ft], cx3[:, ft], VC["frmaa"][:, ft:ft + 1], xc3[:, ft],
                OP.mult, OP.add),
                reads=[K(U_XX), K(U_XA), "vc_frmaa"], writes=[K(U_XF1)])
        dump_fm("d_kx", kx, K(U_XF0), ch)

        if 'G' not in PHASES:
            return
        sig = unit(U_XG)
        wq, wqbuf = piece(ch, "Wrf")
        for tt in range(NTT):
            for nh in range(2):
                QP, qpb = (QD2, "QD2") if (2 * tt + nh) % 2 == 0 \
                    else (QD3, "QD3")
                for kt in range(NFT):
                    sc.op("T", lambda e, tt=tt, nh=nh, kt=kt, wq=wq, QP=QP:
                          e.matmul(
                        QP[:, 0:512],
                        rx[:, kt * TC + tt * 128: kt * TC + (tt + 1) * 128],
                        wq[:, kt * C + nh * 512: kt * C + (nh + 1) * 512],
                        start=(kt == 0), stop=(kt == NFT - 1)),
                        reads=[wqbuf, K(U_XF1)], writes=[qpb])
                sc.op("S", lambda e, tt=tt, nh=nh, QP=QP: e.activation(
                    sig[:, tt * C + nh * 512: tt * C + (nh + 1) * 512],
                    QP[:, 0:512], AF.Sigmoid), reads=[qpb], writes=[K(U_XG)])
        dump_tm("d_sig", sig, K(U_XG), ch)

        if 'H' not in PHASES:
            return
        kkfull = bfa[:, U_R * U: U_R * U + FFN * 4]
        kkkeys = [K(U_R), K(U_K), K(U_V), K(U_G)]
        for q in range(4):
            wq, wqbuf = piece(ch, f"wkf{q}")
            for mt in range(7):
                gmt = q * 7 + mt
                QP, qpb = (QD2, "QD2") if gmt % 2 == 0 else (QD3, "QD3")
                for kt in range(NFT):
                    sc.op("T", lambda e, mt=mt, kt=kt, wq=wq, QP=QP: e.matmul(
                        QP[:, 0:TC],
                        wq[:, kt * 896 + mt * 128: kt * 896 + (mt + 1) * 128],
                        kx3[:, kt], start=(kt == 0), stop=(kt == NFT - 1)),
                        reads=[wqbuf, K(U_XF0)], writes=[qpb])
                sc.op("S", lambda e, gmt=gmt, QP=QP: e.activation(
                    kkfull[:, gmt * TC:(gmt + 1) * TC], QP[:, 0:TC], AF.Relu),
                    reads=[qpb], writes=kkkeys)
                sc.op("V", lambda e, gmt=gmt: e.tensor_tensor(
                    kkfull[:, gmt * TC:(gmt + 1) * TC],
                    kkfull[:, gmt * TC:(gmt + 1) * TC],
                    kkfull[:, gmt * TC:(gmt + 1) * TC], OP.mult),
                    reads=kkkeys, writes=kkkeys)
        dump_fm("d_kk", kkfull, kkkeys[0], ch)
        d2 = unit(U_XX)
        for half in range(2):
            slices = [(tt, nh) for tt in (2 * half, 2 * half + 1)
                      for nh in range(2)]
            for q in range(4):
                wq, wqbuf = piece(ch, f"wvf{q}h{half}")
                for si, (tt, nh) in enumerate(slices):
                    ZP, zpbuf, zoff = [(YY, "YY", 0), (YY, "YY", 512),
                                       (QD0, "QD0", 0), (QD1, "QD1", 0)][si]
                    for kt in range(7):
                        gkt = q * 7 + kt
                        sc.op("T", lambda e, tt=tt, nh=nh, kt=kt, gkt=gkt,
                              wq=wq, q=q, ZP=ZP, zoff=zoff: e.matmul(
                                  ZP[:, zoff:zoff + 512],
                                  kkfull[:, gkt * TC + tt * 128:
                                         gkt * TC + (tt + 1) * 128],
                                  wq[:, kt * C + nh * 512:
                                     kt * C + (nh + 1) * 512],
                                  start=(q == 0 and kt == 0),
                                  stop=(q == 3 and kt == 6)),
                              reads=[wqbuf] + kkkeys, writes=[zpbuf])
            for si, (tt, nh) in enumerate(slices):
                ZP, zpbuf, zoff = [(YY, "YY", 0), (YY, "YY", 512),
                                   (QD0, "QD0", 0), (QD1, "QD1", 0)][si]
                sgl = sig[:, tt * C + nh * 512: tt * C + (nh + 1) * 512]
                dsl = d2[:, tt * C + nh * 512: tt * C + (nh + 1) * 512]
                accsl = acc[:, tt * C + nh * 512: tt * C + (nh + 1) * 512]
                sc.op("V", lambda e, sgl=sgl, dsl=dsl, ZP=ZP, zoff=zoff:
                      e.tensor_tensor(dsl, ZP[:, zoff:zoff + 512], sgl,
                                      OP.mult),
                      reads=[zpbuf, K(U_XG)], writes=[K(U_XX)])
                sc.op("V", lambda e, accsl=accsl, dsl=dsl: e.tensor_tensor(
                    accsl, accsl, dsl, OP.add), reads=[K(U_XX)], writes=["acc"])

        dump_tm("d_d2", d2, K(U_XX), ch)
        dump_tm("d_accH", acc, "acc", ch)
        if 'I' not in PHASES:
            return
        sc.op("Y", lambda e, t0g=t0g: e.dma_start(
            y_out[t0g:t0g + TC, :].rearrange("(tt p) c -> p tt c", p=128),
            acc[:].rearrange("p (tt c) -> p tt c", tt=NTT)),
            reads=["acc"], writes=["yout"], dma=True)

    prologue()
    for ch in range(NCH):
        chunk(ch)
    total_y = {f"Y{i}": sc.cnt[f"Y{i}"] for i in range(8)}

    streams = {e: [] for e in ("T", "V", "S", "G", "Y")}
    for eng, fn, waits, inc, chan in sc.ops:
        streams[eng].append((fn, waits, inc, chan))
    watermark = {e: {} for e in ("T", "V", "S", "G", "Y")}

    def emit_stream(engobj, ename):
        wm = watermark[ename]
        for fn, waits, inc, chan in streams[ename]:
            for (e2, c2) in waits:
                if wm.get(e2, 0) >= c2:
                    continue
                engobj.wait_ge(sems[e2], c2)
                wm[e2] = c2
            inst = fn(engobj)
            inst.then_inc(sems[chan], inc)

    block = ec(nc.Block())

    @block.tensor
    def _(eng):
        emit_stream(eng, "T")

    @block.vector
    def _(eng):
        emit_stream(eng, "V")

    @block.scalar
    def _(eng):
        emit_stream(eng, "S")

    @block.gpsimd
    def _(eng):
        emit_stream(eng, "G")

    @block.sync
    def _(eng):
        emit_stream(eng, "Y")
        for i in range(8):
            if total_y[f"Y{i}"]:
                eng.wait_ge(sems[f"Y{i}"], total_y[f"Y{i}"])

    es.close()
    return nc


_NC_CACHE = None


def kernel(**inputs):
    global _NC_CACHE
    import ml_dtypes
    x = np.asarray(inputs["L0_attn_x"], np.float32)
    B = x.shape[0]

    def bf(a):
        return np.ascontiguousarray(a).astype(ml_dtypes.bfloat16)

    maskU1 = np.triu(np.ones((128, 128), np.float32), 1)
    eye = np.eye(128, dtype=np.float32)

    def tile(v):
        return np.ascontiguousarray(
            np.broadcast_to(np.asarray(v, np.float32), (128, C)))

    common = {
        "ln0_w": inputs["L0_attn_ln0_w"], "ln0_b": inputs["L0_attn_ln0_b"],
        "ln1_w": inputs["L0_attn_ln1_w"], "ln1_b": inputs["L0_attn_ln1_b"],
        "ln2_w": inputs["L0_attn_ln2_w"], "ln2_b": inputs["L0_attn_ln2_b"],
        "x_maa": inputs["L0_attn_x_maa"], "w_maa": inputs["L0_attn_w_maa"],
        "k_maa": inputs["L0_attn_k_maa"], "v_maa": inputs["L0_attn_v_maa"],
        "r_maa": inputs["L0_attn_r_maa"], "g_maa": inputs["L0_attn_g_maa"],
        "tm_w1": inputs["L0_attn_tm_w1"], "tm_w2": inputs["L0_attn_tm_w2"],
        "td_w1": inputs["L0_attn_td_w1"], "td_w2": inputs["L0_attn_td_w2"],
        "time_decay": inputs["L0_attn_time_decay"],
        "time_faaaa": inputs["L0_attn_time_faaaa"],
        "Wr": inputs["L0_attn_Wr"], "Wk": inputs["L0_attn_Wk"],
        "Wv": inputs["L0_attn_Wv"], "Wg": inputs["L0_attn_Wg"],
        "Wo": inputs["L0_attn_Wo"],
        "lnx_w": inputs["L0_attn_lnx_w"], "lnx_b": inputs["L0_attn_lnx_b"],
        "fk_maa": inputs["L0_attn_fk_maa"], "fr_maa": inputs["L0_attn_fr_maa"],
        "Wk_f": inputs["L0_attn_Wk_f"], "Wv_f": inputs["L0_attn_Wv_f"],
        "Wr_f": inputs["L0_attn_Wr_f"],
        "ffn_g": inputs["L0_ffn_ln_gamma"], "ffn_b": inputs["L0_ffn_ln_beta"],
        "fc1_q": inputs["L0_ffn_fc1_q"], "fc1_bias": inputs["L0_ffn_fc1_bias"],
        "fc2_q": inputs["L0_ffn_fc2_q"], "fc2_bias": inputs["L0_ffn_fc2_bias"],
        "cmaskQ": bf(np.concatenate([maskU1, eye, maskU1, eye], 1)),
        "cident": bf(eye),
        "cw0bc": tile(inputs["L0_attn_ln0_w"]),
        "cb0bc": tile(inputs["L0_attn_ln0_b"]),
        "clnxw": tile(inputs["L0_attn_lnx_w"]),
        "clnxb": tile(inputs["L0_attn_lnx_b"]),
        "cfc2bb": tile(inputs["L0_ffn_fc2_bias"]),
        "cfc1s": np.ascontiguousarray(
            np.repeat(np.asarray(inputs["L0_ffn_fc1_scales"], np.float32),
                      128, axis=0)),
        "cfc2s": np.ascontiguousarray(
            np.repeat(np.asarray(inputs["L0_ffn_fc2_scales"], np.float32),
                      128, axis=0)),
    }
    common = {k2: np.ascontiguousarray(np.asarray(v)) for k2, v in common.items()}

    if _NC_CACHE is None:
        _NC_CACHE = build_nc()
    nc = _NC_CACHE
    in_maps = []
    for b in range(B):
        m = dict(common)
        m["x"] = np.ascontiguousarray(x[b])
        in_maps.append(m)
    res = run_bass_kernel_spmd(nc, in_maps, core_ids=list(range(B)))
    out = np.stack([np.asarray(res.results[b]["y"], np.float32)
                    for b in range(B)], axis=0)
    kernel.results = res.results
    return out
